# version 47
# baseline (speedup 1.0000x reference)
import numpy as np
import ml_dtypes
from contextlib import ExitStack

import concourse.bass as bass
import concourse.mybir as mybir
from concourse.bass_utils import run_bass_kernel_spmd

F32 = mybir.dt.float32
BF16 = mybir.dt.bfloat16
I32 = mybir.dt.int32
AF = mybir.ActivationFunctionType
ALU = mybir.AluOpType
AX = mybir.AxisListType

T = 2048
D = 1024
NHP = 8
FF = 3072
NFC = FF // 128
NIC = D // 128
NKT = T // 128
NQC = T // 512
QSCALE = 0.125
WSCALE = 1.0 / 127.0
W2 = (1.0 / 127.0) ** 2
EPS = 1e-5
NCORES = 8

WDSLOT = {0: 21, 1: 22, 2: 23, 3: 14, 4: 15, 5: 16, 6: 17, 7: 18, 8: 19, 9: 20}
for _f in range(10, NFC):
    WDSLOT[_f] = _f - 10
del _f


class Sem:
    def __init__(self, E, nc, name):
        self.h = E(nc.semaphore(name))
        self.n = 0

    def inc(self, instr, k=1):
        instr.then_inc(self.h, k)
        self.n += k
        return self.n


class DmaQ:

    def __init__(self, E, nc, name, n=3):
        self.sems = [E(nc.semaphore(f"{name}{i}")) for i in range(n)]
        self.cnt = [0] * n
        self.i = 0

    def dma(self, e, out, in_, **kw):
        j = self.i % len(self.sems)
        self.i += 1
        if self.cnt[j]:
            e.wait_ge(self.sems[j], self.cnt[j])
        e.dma_start(out=out, in_=in_, **kw).then_inc(self.sems[j], 16)
        self.cnt[j] += 16
        return (self.sems[j], self.cnt[j])

    def all_tokens(self):
        return [(self.sems[j], self.cnt[j])
                for j in range(len(self.sems)) if self.cnt[j]]


def wtok(e, tok):
    e.wait_ge(tok[0], tok[1])


class Prog:

    _ET = {"sync": mybir.EngineType.SP, "scalar": mybir.EngineType.Activation,
           "vector": mybir.EngineType.DVE, "tensor": mybir.EngineType.PE,
           "gpsimd": mybir.EngineType.Pool}

    def __init__(self, nc, block):
        self.nc = nc
        self.block = block
        self.names = {}

    def _deco(self, engname):
        def deco(f):
            nc = self.nc
            if engname not in self.names:
                getattr(self.block, engname)(f)
                self.names[engname] = nc.m.functions[0].blocks[-1].name
            else:
                with nc.body(self.names[engname],
                             valid_engines=list(self._ET.values()),
                             allow_existing_parent=True):
                    f(getattr(nc, engname))
        return deco

    @property
    def sync(self):
        return self._deco("sync")

    @property
    def scalar(self):
        return self._deco("scalar")

    @property
    def vector(self):
        return self._deco("vector")

    @property
    def tensor(self):
        return self._deco("tensor")

    @property
    def gpsimd(self):
        return self._deco("gpsimd")


def build_core(n_seq=2, taps=(), stop_after=None):
    nc = bass.Bass("TRN2", target_bir_lowering=False)
    NSEQ = n_seq

    x_d = nc.dram_tensor("x", [NSEQ * T, D], F32, kind="ExternalInput").ap()
    w_d = {w: nc.dram_tensor("w" + w, [D, D], F32, kind="ExternalInput").ap()
           for w in ("q", "k", "v", "o")}
    bias_d = {w: nc.dram_tensor("b" + w, [1, D], F32, kind="ExternalInput").ap()
              for w in ("q", "k", "v", "o")}
    wgT_d = nc.dram_tensor("wgT", [128, NIC * FF], I32, kind="ExternalInput").ap()
    wuT_d = nc.dram_tensor("wuT", [128, NIC * FF], I32, kind="ExternalInput").ap()
    wdT_d = nc.dram_tensor("wdT", [128, NFC * D], I32, kind="ExternalInput").ap()
    idt_d = nc.dram_tensor("idt", [128, 128], BF16, kind="ExternalInput").ap()
    pt_d = nc.dram_tensor("pt", [128, 128], BF16, kind="ExternalInput").ap()
    cos_d = nc.dram_tensor("cosb", [128, T], BF16, kind="ExternalInput").ap()
    sin_d = nc.dram_tensor("sinb", [128, T], BF16, kind="ExternalInput").ap()
    ones_d = nc.dram_tensor("onesb", [1, 128], BF16, kind="ExternalInput").ap()
    out_d = nc.dram_tensor("out", [NSEQ * T, D], F32, kind="ExternalOutput").ap()

    xnscr_d = nc.dram_tensor("xnscr", [128, NSEQ * NIC * T], BF16, kind="Internal").ap()
    ctxscr_d = nc.dram_tensor("ctxscr", [128, NSEQ * NKT * 1024], BF16, kind="Internal").ap()

    tap_d = {}
    for tname, tshape, tdt in taps:
        tap_d[tname] = nc.dram_tensor("tap_" + tname, list(tshape), tdt,
                                      kind="ExternalOutput").ap()

    ctx = ExitStack()
    E = ctx.enter_context
    with ctx:
        idt = E(nc.sbuf_tensor("idt_s", [128, 128], BF16))
        pt = E(nc.sbuf_tensor("pt_s", [128, 128], BF16))
        cosb = E(nc.sbuf_tensor("cos_s", [128, T], BF16))
        sinb = E(nc.sbuf_tensor("sin_s", [128, T], BF16))
        onesb = E(nc.sbuf_tensor("ones_s", [1, 128], BF16))
        biasb = E(nc.sbuf_tensor("bias_s", [1, 4 * D], BF16))
        BOFF = {"q": 0, "k": D, "v": 2 * D, "o": 3 * D}
        zr = E(nc.sbuf_tensor("zr", [128, 16], F32))
        bfm = E(nc.sbuf_tensor("bfm", [128, 16], F32))
        statsA = E(nc.sbuf_tensor("statsA", [128, 32], F32))
        statsB = E(nc.sbuf_tensor("statsB", [128, 48], F32))

        PS = [E(nc.psum_tensor(f"PS{i}", [128, 1024], F32)) for i in range(3)]
        PC = [E(nc.psum_tensor(f"PC{i}", [128, 512], F32)) for i in range(2)]
        pOO = PS[0][:, 0:512]

        dv = Sem(E, nc, "dv_p")
        sc = Sem(E, nc, "sc_p")
        pe = Sem(E, nc, "pe_p")
        spq = DmaQ(E, nc, "spq", 3)
        gpq = DmaQ(E, nc, "gpq", 3)
        stq = DmaQ(E, nc, "stq", 4)

        block = E(nc.Block())
        P = Prog(nc, block)

        actx = ExitStack()
        A = actx.enter_context
        wa = A(nc.sbuf_tensor("wa", [128, NIC * D], BF16))
        wb = A(nc.sbuf_tensor("wb", [128, NIC * D], BF16))
        xn_fm = A(nc.sbuf_tensor("xn_fm", [128, NIC * T], BF16))
        k_fm = A(nc.sbuf_tensor("k_fm", [128, NHP * T], BF16))
        q_fm = A(nc.sbuf_tensor("q_fm", [128, NHP * 512], BF16))
        vp = A(nc.sbuf_tensor("vp", [128, NKT * 1040], BF16))
        exa = A(nc.sbuf_tensor("exa", [128, 3072], BF16))
        ctxtm = A(nc.sbuf_tensor("ctxtm", [128, 4 * 1024], BF16))
        ctxfm = [A(nc.sbuf_tensor(f"ctxfm{i}", [128, 1024], BF16)) for i in range(2)]
        xldr = [A(nc.sbuf_tensor(f"xldr{i}", [128, D], F32)) for i in range(2)]
        kraw = [A(nc.sbuf_tensor(f"kraw{i}", [128, 512], BF16)) for i in range(2)]
        tmps = [A(nc.sbuf_tensor(f"tmps{i}", [128, 512], BF16)) for i in range(2)]
        tmpc = [A(nc.sbuf_tensor(f"tmpc{i}", [128, 512], BF16)) for i in range(2)]
        spacer2 = A(nc.sbuf_tensor("spacer2", [128, 3840], BF16))
        obuf = [A(nc.sbuf_tensor(f"obuf{i}", [128, D], F32)) for i in range(2)]
        sg = [A(nc.sbuf_tensor(f"sg{i}", [128, 256], BF16)) for i in range(2)]
        hbuf = [A(nc.sbuf_tensor(f"hbuf{i}", [128, 256], BF16)) for i in range(2)]
        hbuf.append(nc.alloc_sbuf_tensor_at("hbuf2", [128, 256], BF16,
                                            offset=nc.lookup_mloc(biasb).addr))
        exa2 = A(nc.sbuf_tensor("exa2", [128, 1024], BF16))
        exa3 = A(nc.sbuf_tensor("exa3", [128, 1024], BF16))

        def exslot(j, lo, n):
            j = j % 5
            if j < 3:
                return exa[:, j * 1024 + lo: j * 1024 + lo + n]
            return (exa2, exa3)[j - 3][:, lo: lo + n]
        xb = [ctxtm[:, i * D:(i + 1) * D] for i in range(4)]
        sqscr = ctxfm[0]

        def _addr(h):
            return nc.lookup_mloc(h).addr

        assert _addr(k_fm) == _addr(xn_fm) + 32768
        assert _addr(q_fm) == _addr(k_fm) + 32768
        assert _addr(vp) == _addr(q_fm) + 8192
        assert _addr(spacer2) + 7680 == _addr(vp) + 24576 + 49152, (_addr(spacer2), _addr(vp))
        wgT_b = nc.alloc_sbuf_tensor_at("wgT_b", [128, NIC * FF], BF16,
                                        offset=_addr(xn_fm))
        wuT_b = nc.alloc_sbuf_tensor_at("wuT_b", [128, NIC * FF], BF16,
                                        offset=_addr(k_fm) + 16384)
        wdT_b = nc.alloc_sbuf_tensor_at("wdT_b", [128, NFC * D], BF16,
                                        offset=_addr(vp) + 24576)
        xnc = [nc.alloc_sbuf_tensor_at(f"xnc{i}", [128, NIC * 256], BF16,
                                       offset=_addr(wa) + i * 4096) for i in range(2)]
        ctx2 = [nc.alloc_sbuf_tensor_at(f"ctx2_{i}", [128, 2 * 1024], BF16,
                                        offset=_addr(wa) + 8192 + i * 4096) for i in range(2)]
        xres2 = [nc.alloc_sbuf_tensor_at(f"xres2_{i}", [128, D], F32,
                                         offset=_addr((cosb, sinb)[i]))
                 for i in range(2)]
        ffnw_tok = {}

        ctok = {}

        @P.sync
        def _(e):
            for nm, dst, src in (("idt", idt, idt_d), ("pt", pt, pt_d),
                                 ("ones", onesb, ones_d)):
                ctok[nm] = spq.dma(e, dst[:], src)

        @P.gpsimd
        def _(e):
            for w in ("q", "k", "v", "o"):
                ctok["b" + w] = gpq.dma(e, biasb[:, BOFF[w]:BOFF[w] + D], bias_d[w])
            ffnw_tok["wdA"] = gpq.dma(e, wdT_b[:, 21 * D:24 * D], wdT_d[:, 0:3 * D])

        wload_tok = {}

        def load_w(buf, name, key, wait_pe=0, split=None):
            @P.gpsimd
            def _(e):
                if wait_pe:
                    e.wait_ge(pe.h, wait_pe)
                dst = buf[:].rearrange("p (ic o) -> p ic o", ic=NIC)
                src = w_d[key].rearrange("(ic p) o -> p ic o", p=128)
                if split is None:
                    wload_tok[name] = gpq.dma(e, dst, src)
                else:
                    lo = 0
                    for ci, hi in enumerate(split):
                        wload_tok[(name, ci)] = gpq.dma(
                            e, dst[:, :, lo:hi], src[:, :, lo:hi])
                        lo = hi

        load_w(wa, "wk0", "k")
        load_w(wb, "wv0", "v")

        @P.tensor
        def _(e):
            wtok(e, ctok["bq"])
            wtok(e, ctok["bk"])
            wtok(e, ctok["ones"])
            i = None
            for wi, w in enumerate(("q", "k")):
                for oc in range(NHP):
                    i = e.matmul(PS[2][:, wi * 8 + oc: wi * 8 + oc + 1],
                                 biasb[0:1, BOFF[w] + oc * 128: BOFF[w] + oc * 128 + 128],
                                 onesb[0:1, 0:1], start=True, stop=True)
            pe.inc(i)
        bfm_pe = pe.n

        @P.vector
        def _(e):
            e.wait_ge(pe.h, bfm_pe)
            i = e.tensor_copy(bfm[:], PS[2][:, 0:16])
            dv.inc(i)
        bfm_dv = dv.n

        NTCH = NSEQ * T // 256
        xnc_tok = []
        ctx_ld_tok = []
        xres_tok = []

        def _ffn_act_loads(e, u):
            s_idx = (u * 256) // T
            w_idx = (u * 256) % T
            src = xnscr_d[:, s_idx * NIC * T: (s_idx + 1) * NIC * T]\
                .rearrange("p (ic t) -> p ic t", ic=NIC)[:, :, w_idx:w_idx + 256]
            xnc_tok.append(spq.dma(
                e, xnc[u % 2][:].rearrange("p (ic t) -> p ic t", ic=NIC), src))
            wtok(e, ctx_tok[2 * u])
            wtok(e, ctx_tok[2 * u + 1])
            ctx_ld_tok.append(spq.dma(
                e, ctx2[u % 2][:], ctxscr_d[:, u * 2048:(u + 1) * 2048]))

        exp_sc_all = []
        sco_pe_all = []
        pv_pe_all = []
        div_dv_all = []
        div_ab = ([], [])
        ocev_dv_all = []
        otp_pe_all = []
        ctx_tok = {}
        ln_sq_sc = []
        state = {"last_rope_pe": 0, "xn_read_pe": 0, "tpc_pe_last": 0,
                 "stats_sc": 0}

        for s in range(NSEQ):
            ln_x_tok = []
            ln_sq_sc_l = []
            ln_red_dv = []
            ln_nmr_dv = []
            ln_tp_pe = []
            ln_evac_dv = []
            ln_x2_tok = []

            lnring = [xldr[0], xldr[1], obuf[0], obuf[1]]
            for tt in range(NKT):
                par = tt % 4

                @P.sync
                def _(e, tt=tt, par=par, s=s):
                    if s == 0 and tt == 4 and "cos" not in ctok:
                        ctok["cos"] = spq.dma(e, cosb[:], cos_d)
                        ctok["sin"] = spq.dma(e, sinb[:], sin_d)
                    if s > 0 and tt < 4:
                        e.wait_ge(sc.h, state["xldr_sc"])
                        e.wait_ge(dv.h, state["xldr_dv"])
                    if tt >= 4:
                        e.wait_ge(sc.h, ln_sq_sc_l[tt - 4])
                        e.wait_ge(dv.h, ln_red_dv[tt - 4])
                    ln_x_tok.append(spq.dma(
                        e, lnring[par][:],
                        x_d[s * T + tt * 128: s * T + tt * 128 + 128, :]))

                @P.scalar
                def _(e, tt=tt, par=par):
                    wtok(e, ln_x_tok[tt])
                    if tt == 0:
                        if ocev_dv_all:
                            wtok(e, ocev_dv_all[-1])
                        if ctx_tok:
                            for qtg in sorted(ctx_tok)[-2:]:
                                wtok(e, ctx_tok[qtg])
                    else:
                        e.wait_ge(sc.h, ln_sq_sc_l[tt - 1])
                    i = e.activation(sqscr[:], lnring[par][:], AF.Square,
                                     accum_out=statsA[:, 16 + tt:17 + tt])
                    sc.inc(i)
                    ln_sq_sc.append(sc.n)
                    ln_sq_sc_l.append(sc.n)

                @P.vector
                def _(e, tt=tt, par=par):
                    wtok(e, ln_x_tok[tt])
                    if tt == 0 and s > 0:
                        e.wait_ge(sc.h, state["stats_sc"])
                    i = e.reduce_sum(statsA[:, tt:tt + 1], lnring[par][:], axis=AX.X)
                    dv.inc(i)
                    ln_red_dv.append(dv.n)

            @P.vector
            def _(e):
                e.wait_ge(dv.h, ln_red_dv[-1])
                i = e.tensor_scalar_mul(statsB[:, 0:16], statsA[:, 0:16], 1.0 / D)
                dv.inc(i)
                e.wait_ge(dv.h, dv.n)
                i = e.tensor_mul(statsB[:, 16:32], statsB[:, 0:16], statsB[:, 0:16])
                dv.inc(i)
                e.wait_ge(sc.h, ln_sq_sc_l[-1])
                e.wait_ge(dv.h, dv.n)
                i = e.scalar_tensor_tensor(statsA[:, 0:16], statsA[:, 16:32], 1.0 / D,
                                           statsB[:, 16:32], ALU.mult, ALU.subtract)
                dv.inc(i)
                e.wait_ge(dv.h, dv.n)
                i = e.tensor_scalar_add(statsB[:, 16:32], statsA[:, 0:16], EPS)
                dv.inc(i)
            var_dv = dv.n

            @P.scalar
            def _(e, var_dv=var_dv):
                e.wait_ge(dv.h, var_dv)
                i = e.activation(statsA[:, 0:16], statsB[:, 16:32], AF.Ln)
                sc.inc(i)
                e.wait_ge(sc.h, sc.n)
                i = e.activation(statsB[:, 32:48], statsA[:, 0:16], AF.Exp, scale=-0.5)
                sc.inc(i)
            rstd_sc = sc.n
            state["stats_sc"] = rstd_sc

            for tt in range(NKT):
                par = tt % 4

                @P.sync
                def _(e, tt=tt, par=par, s=s):
                    if tt < 4:
                        e.wait_ge(sc.h, ln_sq_sc_l[12 + par])
                        e.wait_ge(dv.h, ln_red_dv[12 + par])
                    else:
                        e.wait_ge(dv.h, ln_nmr_dv[tt - 4])
                    ln_x2_tok.append(spq.dma(
                        e, lnring[par][:],
                        x_d[s * T + tt * 128: s * T + tt * 128 + 128, :]))

                @P.vector
                def _(e, tt=tt, par=par, rstd_sc=rstd_sc):
                    wtok(e, ln_x2_tok[tt])
                    if tt < 4:
                        e.wait_ge(sc.h, rstd_sc)
                        e.wait_ge(pe.h, state["tpc_pe_last"])
                        for qtg in sorted(ctx_tok)[-4:]:
                            wtok(e, ctx_tok[qtg])
                    else:
                        e.wait_ge(pe.h, ln_tp_pe[tt - 4])
                    i = e.tensor_scalar(xb[par][:], lnring[par][:],
                                        statsB[:, tt:tt + 1], statsB[:, 32 + tt:33 + tt],
                                        ALU.subtract, ALU.mult)
                    dv.inc(i)
                    ln_nmr_dv.append(dv.n)

                @P.tensor
                def _(e, tt=tt, par=par, s=s):
                    e.wait_ge(dv.h, ln_nmr_dv[tt])
                    if tt >= 1:
                        wtok(e, ln_evac_dv[tt - 1])
                    else:
                        if ocev_dv_all:
                            wtok(e, ocev_dv_all[-1])
                        if s == 0:
                            wtok(e, ctok["idt"])
                            e.wait_ge(dv.h, bfm_dv)
                    pst = PS[2]
                    i = None
                    for ic in range(NIC):
                        i = e.matmul(pst[:, ic * 128:(ic + 1) * 128],
                                     xb[par][:, ic * 128:(ic + 1) * 128], idt[:],
                                     start=True, stop=True)
                    pe.inc(i)
                    ln_tp_pe.append(pe.n)

                @P.scalar
                def _(e, tt=tt, s=s):
                    e.wait_ge(pe.h, ln_tp_pe[tt])
                    if tt == 0 and s > 0:
                        e.wait_ge(pe.h, state["xn_read_pe"])
                        wtok(e, state[("xnscr_tok", s - 1)])
                    pst = PS[2]
                    dst = xn_fm[:].rearrange("p (ic t) -> p ic t", ic=NIC)[:, :, tt * 128:tt * 128 + 128]
                    src = pst.rearrange("p (ic c) -> p ic c", ic=NIC)
                    i = e.copy(dst, src)
                    sc.inc(i)
                    ln_evac_dv.append((sc.h, sc.n))

            xn_tok = ln_evac_dv[-1]
            state["xldr_sc"] = ln_sq_sc_l[-1]
            state["xldr_dv"] = ln_nmr_dv[-1]

            @P.sync
            def _(e, s=s, xn_tok=xn_tok):
                wtok(e, xn_tok)
                state[("xnscr_tok", s)] = spq.dma(
                    e, xnscr_d[:, s * NIC * T:(s + 1) * NIC * T], xn_fm[:])

            if s == NSEQ - 1 and stop_after is None:
                @P.gpsimd
                def _(e, xsc=state["xldr_sc"], xdv=state["xldr_dv"]):
                    e.wait_ge(sc.h, xsc)
                    e.wait_ge(dv.h, xdv)
                    ffnw_tok["wdB"] = gpq.dma(
                        e, wdT_b[:, 14 * D:17 * D], wdT_d[:, 3 * D:6 * D])

            if stop_after == "ln" and s == NSEQ - 1:
                break

            kv_proj_pe = []
            kv_tmp_dv = []
            kv_rope_pe = []
            kv_evac_dv = []
            v_proj_pe = []
            v_evac_dv = []
            kraw_sc = []
            punit_free = []

            @P.vector
            def _(e, s=s):
                if ("vp_read_pe", s - 1) in state:
                    e.wait_ge(pe.h, state["vp_read_pe", s - 1])
                i = e.memset(vp[:].rearrange("p (kt h c) -> p kt h c",
                                             kt=NKT, h=16)[:, :, :, 64:65], 1.0)
                dv.inc(i)
            vones_dv = dv.n

            def emit_k_rope(jt, s=s):
                par = jt % 2
                pR = PS[1][:, par * 512:(par + 1) * 512]

                @P.tensor
                def _(e, par=par, pR=pR, jt=jt, s=s):
                    e.wait_ge(dv.h, kv_tmp_dv[jt])
                    if jt >= 2:
                        wtok(e, kv_evac_dv[jt - 2])
                    if jt == 0 and s == 0:
                        wtok(e, ctok["pt"])
                    e.matmul(pR, idt[:], tmpc[par][:], start=True, stop=False)
                    i = e.matmul(pR, pt[:], tmps[par][:], start=False, stop=True)
                    pe.inc(i)
                    kv_rope_pe.append(pe.n)

                tc, hp = jt // NHP, jt % NHP

                @P.scalar
                def _(e, hp=hp, tc=tc, pR=pR, jt=jt, s=s):
                    e.wait_ge(pe.h, kv_rope_pe[jt])
                    if s > 0 and jt == 0:
                        e.wait_ge(pe.h, state["kfm_read_pe", s - 1])
                    i = e.copy(k_fm[:, hp * T + tc * 512: hp * T + tc * 512 + 512], pR)
                    sc.inc(i)
                    kv_evac_dv.append((sc.h, sc.n))

            uidx = 0
            kidx = 0
            vidx = 0
            for tc in range(4):
                for u in range(16):
                    par = uidx % 2
                    pP = PS[0][:, par * 512:(par + 1) * 512]
                    is_k = (u % 2 == 1)
                    if is_k:
                        hp = u // 2
                        it = kidx

                        @P.tensor
                        def _(e, hp=hp, tc=tc, pP=pP, it=it, uidx=uidx, s=s,
                              lev=list(ln_evac_dv)):
                            if it == 0:
                                wtok(e, wload_tok[f"wk{s}"])
                                wtok(e, ctok["bk"])
                                if s == 0:
                                    wtok(e, ctok["ones"])
                            wtok(e, lev[4 * tc + 3])
                            if uidx >= 2:
                                wtok(e, punit_free[uidx - 2])
                            i = None
                            for ic in range(NIC):
                                i = e.matmul(pP, wa[:, ic * D + hp * 128: ic * D + hp * 128 + 128],
                                             xn_fm[:, ic * T + tc * 512: ic * T + tc * 512 + 512],
                                             start=(ic == 0), stop=(ic == NIC - 1))
                            pe.inc(i)
                            kv_proj_pe.append(pe.n)

                        @P.scalar
                        def _(e, hp=hp, par=par, pP=pP, it=it, s=s):
                            e.wait_ge(pe.h, kv_proj_pe[it])
                            if it >= 2:
                                e.wait_ge(dv.h, kv_tmp_dv[it - 2])
                            if it == 0 and s == 0:
                                e.wait_ge(dv.h, bfm_dv)
                            i = e.activation(kraw[it % 2][:], pP, AF.Identity,
                                             bias=bfm[:, 8 + hp:9 + hp])
                            sc.inc(i)
                            kraw_sc.append(sc.n)
                            punit_free.append((sc.h, sc.n))

                        @P.vector
                        def _(e, tc=tc, it=it, s=s):
                            e.wait_ge(sc.h, kraw_sc[it])
                            if it >= 2:
                                e.wait_ge(pe.h, kv_rope_pe[it - 2])
                            elif state["last_rope_pe"]:
                                e.wait_ge(pe.h, state["last_rope_pe"])
                            if it == 0 and s == 0:
                                wtok(e, ctok["cos"])
                                wtok(e, ctok["sin"])
                            i = e.tensor_mul(tmps[it % 2][:], kraw[it % 2][:],
                                             sinb[:, tc * 512:(tc + 1) * 512])
                            dv.inc(i)
                            i = e.tensor_mul(tmpc[it % 2][:], kraw[it % 2][:],
                                             cosb[:, tc * 512:(tc + 1) * 512])
                            dv.inc(i)
                            kv_tmp_dv.append(dv.n)

                        if kidx >= 1:
                            emit_k_rope(kidx - 1)
                        kidx += 1
                    else:
                        tt = tc * 4 + u // 4
                        j = (u // 2) % 2
                        it = vidx

                        @P.tensor
                        def _(e, tt=tt, j=j, pP=pP, it=it, uidx=uidx, s=s,
                              lev=list(ln_evac_dv)):
                            if it == 0:
                                wtok(e, wload_tok[f"wv{s}"])
                                wtok(e, ctok["bv"])
                                if s == 0:
                                    wtok(e, ctok["ones"])
                            wtok(e, lev[tt])
                            if uidx >= 2:
                                wtok(e, punit_free[uidx - 2])
                            i = None
                            for ic in range(NIC):
                                i = e.matmul(pP, xn_fm[:, ic * T + tt * 128: ic * T + tt * 128 + 128],
                                             wb[:, ic * D + j * 512: ic * D + j * 512 + 512],
                                             start=(ic == 0), stop=False)
                            i = e.matmul(pP, onesb[0:1, 0:128],
                                         biasb[0:1, BOFF["v"] + j * 512: BOFF["v"] + j * 512 + 512],
                                         start=False, stop=True)
                            pe.inc(i)
                            v_proj_pe.append(pe.n)

                        @P.scalar
                        def _(e, tt=tt, j=j, pP=pP, it=it, vones_dv=vones_dv):
                            e.wait_ge(pe.h, v_proj_pe[it])
                            e.wait_ge(dv.h, vones_dv)
                            dst = vp[:].rearrange("p (kt h c) -> p kt h c",
                                                  kt=NKT, h=16)[:, tt, 8 * j:8 * j + 8, 0:64]
                            src = pP.rearrange("p (h c) -> p h c", h=8)
                            i = e.copy(dst, src)
                            sc.inc(i)
                            v_evac_dv.append((sc.h, sc.n))
                            punit_free.append((sc.h, sc.n))

                        vidx += 1
                    uidx += 1
            emit_k_rope(kidx - 1)
            state["last_rope_pe"] = kv_rope_pe[-1]

            v_done_tok = v_evac_dv[-1]
            load_w(wa, f"wq{s}", "q", wait_pe=kv_proj_pe[-1],
                   split=tuple(range(128, 1025, 128)))
            if s == NSEQ - 1:
                @P.gpsimd
                def _(e):
                    e.wait_ge(pe.h, v_proj_pe[-1])
                    ffnw_tok["wo"] = gpq.dma(
                        e, wb[:].rearrange("p (ic o) -> p ic o", ic=NIC),
                        w_d["o"].rearrange("(ic p) o -> p ic o", p=128))

            if stop_after == "kv" and s == NSEQ - 1:
                break

            state["kv_evac_last"] = kv_evac_dv[-1]
            qp = {}

            def new_qpset(qc_t, prev):
                qp[qc_t] = {"raw": [], "kraw": [], "tmp": [], "rope": [],
                            "evac": [], "prev": prev}

            def emit_qproj_raw(qc_t, h, pc, s=s, sa=False):
                L = qp[qc_t]
                pP2 = (PS[0][:, (h % 2) * 512:(h % 2) * 512 + 512] if sa
                       else PS[2][:, 0:512])
                dep = 2 if sa else 1

                @P.tensor
                def _(e, qc_t=qc_t, h=h, L=L, sa=sa, dep=dep, pP2=pP2, pc=pc):
                    if pc[0] == 0:
                        wtok(e, wload_tok[(f"wq{s}", h)])
                        if h == 0:
                            if sa:
                                wtok(e, v_done_tok)
                            else:
                                if L["prev"]["kraw_dv"]:
                                    e.wait_ge(dv.h, L["prev"]["kraw_dv"])
                                if L["prev"]["ps2_sc"]:
                                    wtok(e, L["prev"]["ps2_sc"])
                        if h >= dep:
                            e.wait_ge(dv.h, L["kraw"][h - dep])
                    i = None
                    for ic in range(pc[0], pc[1]):
                        i = e.matmul(pP2,
                                     wa[:, ic * D + h * 128: ic * D + h * 128 + 128],
                                     xn_fm[:, ic * T + qc_t * 512: ic * T + qc_t * 512 + 512],
                                     start=(ic == 0), stop=(ic == NIC - 1))
                    if pc[1] == NIC:
                        pe.inc(i)
                        L["raw"].append(pe.n)

                @P.vector
                def _(e, qc_t=qc_t, h=h, L=L, pP2=pP2, pc=pc):
                    if pc[1] != NIC:
                        return
                    par = h % 2
                    e.wait_ge(pe.h, L["raw"][h])
                    if h >= 2:
                        e.wait_ge(pe.h, L["rope"][h - 2])
                    elif L["prev"]["rope_pe"]:
                        e.wait_ge(pe.h, L["prev"]["rope_pe"])
                    i = e.scalar_tensor_tensor(tmps[par][:], pP2, bfm[:, h:h + 1],
                                               sinb[:, qc_t * 512:(qc_t + 1) * 512],
                                               ALU.add, ALU.mult)
                    dv.inc(i)
                    i = e.scalar_tensor_tensor(tmpc[par][:], pP2, bfm[:, h:h + 1],
                                               cosb[:, qc_t * 512:(qc_t + 1) * 512],
                                               ALU.add, ALU.mult)
                    dv.inc(i)
                    L["tmp"].append(dv.n)
                    L["kraw"].append(dv.n)

            def emit_qproj_b(qc_t, h, qfm_free, sa=False):
                L = qp[qc_t]
                pR2 = (PS[1][:, (h % 2) * 512:(h % 2) * 512 + 512] if sa
                       else PS[2][:, 512:1024])
                dep = 2 if sa else 1

                @P.tensor
                def _(e, h=h, L=L, sa=sa, dep=dep, pR2=pR2):
                    e.wait_ge(dv.h, L["tmp"][h])
                    if h >= dep:
                        e.wait_ge(dv.h, L["evac"][h - dep])
                    elif sa:
                        wtok(e, kv_evac_dv[-1])
                    elif L["prev"]["evac_dv"]:
                        e.wait_ge(dv.h, L["prev"]["evac_dv"])
                    e.matmul(pR2, idt[:], tmpc[h % 2][:], start=True, stop=False)
                    i = e.matmul(pR2, pt[:], tmps[h % 2][:], start=False, stop=True)
                    pe.inc(i)
                    L["rope"].append(pe.n)

                @P.vector
                def _(e, h=h, L=L, qfm_free=qfm_free, pR2=pR2):
                    e.wait_ge(pe.h, L["rope"][h])
                    if qfm_free:
                        e.wait_ge(pe.h, qfm_free)
                    i = e.tensor_scalar_mul(q_fm[:, h * 512:(h + 1) * 512], pR2, 1.0)
                    dv.inc(i)
                    L["evac"].append(dv.n)

            def emit_ctx_tp(qc_t, qt, ocl, div_tok, s=s, pc=(0, NIC)):
                qtg = (s * NQC + qc_t) * 4 + qt

                @P.tensor
                def _(e, qt=qt, div_tok=div_tok, ocl=ocl, pc=pc):
                    if pc[0] == 0:
                        e.wait_ge(dv.h, div_tok)
                        if qt >= 1:
                            wtok(e, ocl[qt - 1])
                        elif ocev_dv_all:
                            wtok(e, ocev_dv_all[-1])
                        else:
                            wtok(e, ln_evac_dv[-1])
                    i = None
                    for ic in range(pc[0], pc[1]):
                        i = e.matmul(PS[2][:, ic * 128:(ic + 1) * 128],
                                     ctxtm[:, qt * 1024 + ic * 128:
                                           qt * 1024 + ic * 128 + 128],
                                     idt[:], start=True, stop=True)
                    if pc[1] == NIC:
                        pe.inc(i)
                        otp_pe_all.append(pe.n)

                if pc[1] != NIC:
                    return

                @P.vector
                def _(e, qt=qt, qtg=qtg):
                    e.wait_ge(pe.h, otp_pe_all[-1])
                    if (qtg - 4) in ctx_tok:
                        wtok(e, ctx_tok[qtg - 4])
                    i = e.tensor_scalar_mul(ctxtm[:, qt * 1024:(qt + 1) * 1024],
                                            PS[2][:], 1.0)
                    dv.inc(i)
                    tok = (dv.h, dv.n)
                    ocev_dv_all.append(tok)
                    ocl.append(tok)

                @P.sync
                def _(e, qt=qt, qtg=qtg, ocl=ocl):
                    wtok(e, ocl[-1])
                    ctx_tok[qtg] = spq.dma(
                        e, ctxscr_d[:, qtg * 1024:(qtg + 1) * 1024],
                        ctxtm[:, qt * 1024:(qt + 1) * 1024])

                if qt == 3:
                    state["tpc_pe_last"] = otp_pe_all[-1]

            new_qpset(0, {"kraw_dv": 0, "ps2_sc": None,
                          "rope_pe": state["last_rope_pe"], "evac_dv": 0})
            qfm0 = state.get(("kfm_read_pe", s - 1), 0)
            for h in range(NHP):
                emit_qproj_raw(0, h, (0, NIC), sa=True)
                if h >= 1:
                    emit_qproj_b(0, h - 1, qfm0, sa=True)
            emit_qproj_b(0, NHP - 1, qfm0, sa=True)

            for qc in range(NQC):
                tp_ocl = []
                tp_div = div_dv_all[-1] if qc >= 1 else 0
                if s == NSEQ - 1 and qc == NQC - 1 and stop_after is None:
                    @P.gpsimd
                    def _(e, s=s, qpt=qp[NQC - 1]["raw"][-1],
                          qr=qp[NQC - 1]["rope"][-1],
                          xsc=state["xldr_sc"], xdv=state["xldr_dv"]):
                        e.wait_ge(pe.h, qpt)
                        wtok(e, state[("xnscr_tok", s)])
                        ffnw_tok["wg0"] = gpq.dma(
                            e, wgT_b[:, 0:16 * 1024], wgT_d[:, 0:16 * 1024])
                        e.wait_ge(pe.h, qr)
                        e.wait_ge(sc.h, xsc)
                        e.wait_ge(dv.h, xdv)
                        ffnw_tok["wdC"] = gpq.dma(
                            e, wdT_b[:, 17 * D:21 * D], wdT_d[:, 6 * D:10 * D])

                def emit_pv(hp, g, gi):
                    halves = ((0, 1), (2, 3)) if g == 0 else ((0, 1, 2, 3),)
                    for hi, qts in enumerate(halves):
                        @P.tensor
                        def _(e, hp=hp, g=g, gi=gi, hi=hi, qts=qts):
                            e.wait_ge(sc.h, exp_sc_all[gi])
                            if g == 0 and div_ab[hi]:
                                e.wait_ge(dv.h, div_ab[hi][-1])
                            i = None
                            for h2 in range(2):
                                for qt in qts:
                                    i = e.matmul(
                                        PC[qt // 2][:, (qt % 2) * 130 + h2 * 65:
                                                    (qt % 2) * 130 + h2 * 65 + 65],
                                        exslot(gi, h2 * 512 + qt * 128, 128),
                                        vp[:, g * 1040 + (2 * hp + h2) * 65:
                                           g * 1040 + (2 * hp + h2) * 65 + 65],
                                        start=(g == 0 and h2 == 0 and qt % 2 == 0),
                                        stop=(g == NKT - 1 and h2 == 1 and qt % 2 == 1))
                            if qts[-1] == 3:
                                pe.inc(i)
                                pv_pe_all.append(pe.n)

                def emit_divide(hp_t):
                    for i2 in range(2):
                        @P.vector
                        def _(e, hp=hp_t, i2=i2):
                            if i2 == 0:
                                e.wait_ge(pe.h, pv_pe_all[-1])
                                if hp == 0:
                                    e.wait_ge(pe.h, state["tpc_pe_last"])
                                    for qtg in sorted(ctx_tok)[-4:]:
                                        wtok(e, ctx_tok[qtg])
                            src = PC[i2][:, 0:260].rearrange("p (q2 h2 c) -> p q2 h2 c",
                                                             q2=2, h2=2)[:, :, :, 64:65]
                            dst = zr[:, 4 * i2:4 * i2 + 4].rearrange("p (a b c) -> p a b c",
                                                                     a=2, b=2)
                            i = e.tensor_copy(dst, src)
                            dv.inc(i)
                            e.wait_ge(dv.h, dv.n)
                            i = e.reciprocal(zr[:, 8 + 4 * i2:12 + 4 * i2],
                                             zr[:, 4 * i2:4 * i2 + 4])
                            dv.inc(i)
                            e.wait_ge(dv.h, dv.n)
                            for qt in (2 * i2, 2 * i2 + 1):
                                for h2 in range(2):
                                    zcol = 8 + (qt // 2) * 4 + (qt % 2) * 2 + h2
                                    i = e.tensor_scalar(
                                        ctxtm[:, qt * 1024 + (2 * hp + h2) * 64:
                                              qt * 1024 + (2 * hp + h2) * 64 + 64],
                                        PC[qt // 2][:, (qt % 2) * 130 + h2 * 65:
                                                    (qt % 2) * 130 + h2 * 65 + 64],
                                        zr[:, zcol:zcol + 1], None, ALU.mult)
                                    dv.inc(i)
                            div_ab[i2].append(dv.n)
                    div_dv_all.append(dv.n)

                for hp in range(NHP):
                    for g in range(NKT):
                        gi = len(sco_pe_all)

                        @P.tensor
                        def _(e, hp=hp, g=g, gi=gi, qc=qc, v_done_tok=v_done_tok):
                            if gi >= 2:
                                e.wait_ge(sc.h, exp_sc_all[gi - 2])
                            if g == 0:
                                e.wait_ge(dv.h, qp[qc]["evac"][hp])
                            if qc == 0 and hp == 0:
                                if g == 0:
                                    e.wait_ge(dv.h, qp[0]["kraw"][-1])
                                elif g == 1:
                                    e.wait_ge(dv.h, qp[0]["evac"][-1])
                            ps_t = PS[gi % 2]
                            e.matmul(ps_t[:, 0:512],
                                     k_fm[0:64, hp * T + g * 128: hp * T + g * 128 + 128],
                                     q_fm[0:64, hp * 512:(hp + 1) * 512],
                                     start=True, stop=True, tile_position=(0, 0))
                            i = e.matmul(ps_t[:, 512:1024],
                                         k_fm[64:128, hp * T + g * 128: hp * T + g * 128 + 128],
                                         q_fm[64:128, hp * 512:(hp + 1) * 512],
                                         start=True, stop=True, tile_position=(64, 0))
                            pe.inc(i)
                            sco_pe_all.append(pe.n)

                        @P.scalar
                        def _(e, g=g, gi=gi):
                            e.wait_ge(pe.h, sco_pe_all[gi])
                            if gi >= 5:
                                e.wait_ge(pe.h, pv_pe_all[gi - 5])
                            i = e.activation(exslot(gi, 0, 1024),
                                             PS[gi % 2][:], AF.Exp, scale=QSCALE)
                            sc.inc(i)
                            exp_sc_all.append(sc.n)

                        if g == 0 and hp >= 1:
                            emit_pv(hp - 1, NKT - 4, len(sco_pe_all) - 5)
                            emit_pv(hp - 1, NKT - 3, len(sco_pe_all) - 4)
                            if qc + 1 < NQC:
                                emit_qproj_b(qc + 1, hp - 1, sco_pe_all[-2])
                        elif g == 1 and hp >= 1:
                            emit_pv(hp - 1, NKT - 2, len(sco_pe_all) - 4)
                            emit_pv(hp - 1, NKT - 1, len(sco_pe_all) - 3)
                            emit_divide(hp - 1)
                        elif g >= 4:
                            emit_pv(hp, g - 4, gi - 4)
                        if hp == 0 and qc >= 1 and 1 <= g <= 8:
                            for k in (2 * (g - 1), 2 * (g - 1) + 1):
                                emit_ctx_tp(qc - 1, k // 4, tp_ocl, tp_div,
                                            pc=(2 * (k % 4), 2 * (k % 4) + 2))
                        if 9 <= g <= 12 and qc + 1 < NQC:
                            if hp == 0 and g == 9:
                                if qc == 0:
                                    new_qpset(1, {
                                        "kraw_dv": 0,
                                        "ps2_sc": ln_evac_dv[-1],
                                        "rope_pe": qp[0]["rope"][-1],
                                        "evac_dv": 0})
                                else:
                                    new_qpset(qc + 1, {
                                        "kraw_dv": qp[qc]["kraw"][-1],
                                        "ps2_sc": ocev_dv_all[-1],
                                        "rope_pe": qp[qc]["rope"][-1],
                                        "evac_dv": qp[qc]["evac"][-1]})
                            emit_qproj_raw(qc + 1, hp, (2 * (g - 9), 2 * (g - 9) + 2))

                for gt in (4, 3, 2, 1):
                    emit_pv(NHP - 1, NKT - gt, len(sco_pe_all) - gt)
                if qc + 1 < NQC:
                    emit_qproj_b(qc + 1, NHP - 1, sco_pe_all[-1])
                emit_divide(NHP - 1)

                state["kfm_read_pe", s] = sco_pe_all[-1]
                state["vp_read_pe", s] = pv_pe_all[-1]
                if qc == NQC - 1:
                    state["xn_read_pe"] = qp[qc]["raw"][-1]
                    state["last_rope_pe"] = qp[qc]["rope"][-1]

                if s == NSEQ - 1 and qc == NQC - 1 and stop_after is None:
                    @P.gpsimd
                    def _(e, sco=sco_pe_all[-1]):
                        e.wait_ge(pe.h, sco)
                        ffnw_tok["wu0"] = gpq.dma(
                            e, wuT_b[:, 0:8 * 1024], wuT_d[:, 0:8 * 1024])
                        ffnw_tok["wu1"] = gpq.dma(
                            e, wuT_b[:, 8 * 1024:12 * 1024], wuT_d[:, 8 * 1024:12 * 1024])
                        ffnw_tok["wg1"] = gpq.dma(
                            e, wgT_b[:, 16 * 1024:24 * 1024], wgT_d[:, 16 * 1024:24 * 1024])

                    @P.sync
                    def _(e, xnr=state["xn_read_pe"], tmp3=qp[NQC - 1]["tmp"][-1]):
                        e.wait_ge(pe.h, xnr)
                        wtok(e, state[("xnscr_tok", 0)])
                        _ffn_act_loads(e, 0)
                        if NTCH > 1:
                            _ffn_act_loads(e, 1)
                        e.wait_ge(dv.h, tmp3)
                        for th in range(2):
                            xres_tok.append(spq.dma(
                                e, xres2[th][:], x_d[th * 128:(th + 1) * 128, :]))

                if qc == NQC - 1:
                    ocl3 = []
                    dt3 = div_dv_all[-1]
                    for qt in range(4):
                        emit_ctx_tp(qc, qt, ocl3, dt3)

            if s + 1 < NSEQ:
                load_w(wa, f"wk{s + 1}", "k", wait_pe=state["xn_read_pe"])
                load_w(wb, f"wv{s + 1}", "v", wait_pe=v_proj_pe[-1])

        if stop_after in ("ln", "kv"):
            _emit_taps(P, tap_d, taps, locals(), dv, sc, pe, stq)
            actx.close()
            return nc

        @P.gpsimd
        def _(e):
            e.wait_ge(pe.h, pv_pe_all[-1])
            e.wait_ge(sc.h, exp_sc_all[-1])
            e.wait_ge(pe.h, otp_pe_all[-1])
            e.wait_ge(dv.h, dv.n)
            for qtg in sorted(ctx_tok)[-4:]:
                wtok(e, ctx_tok[qtg])
            ffnw_tok["wdD"] = gpq.dma(
                e, wdT_b[:, 0:14 * D], wdT_d[:, 10 * D:24 * D])
            ffnw_tok["wu2"] = gpq.dma(
                e, wuT_b[:, 12 * 1024:24 * 1024], wuT_d[:, 12 * 1024:24 * 1024])

        actx.close()

        attn_pe = pe.n
        attn_dv = dv.n
        attn_sc = sc.n
        sc_base_ffn = sc.n

        ommf_pe = []
        gate_pe = []
        up_pe = []
        down_pe = []
        silu_sc = []
        h_dv = []
        fevac_dv = []
        out_tok = []

        for tc in range(NTCH):
            if tc >= 1:
                @P.sync
                def _(e, tc=tc):
                    u = tc + 1
                    if u < NTCH:
                        e.wait_ge(pe.h, up_pe[(tc - 1) * NFC + NFC - 1])
                        e.wait_ge(pe.h, ommf_pe[tc - 1])
                        _ffn_act_loads(e, u)
                    for th in range(2):
                        j = 2 * tc + th
                        e.wait_ge(dv.h, fevac_dv[j - 2])
                        row = tc * 256 + th * 128
                        xres_tok.append(spq.dma(
                            e, xres2[j % 2][:], x_d[row:row + 128, :]))


            for fc in range(NFC):
                gu = PC[fc % 2]
                uu = PS[2][:, (fc % 2) * 512:(fc % 2) * 512 + 256]
                gidx = tc * NFC + fc

                @P.tensor
                def _(e, tc=tc, fc=fc, gu=gu, uu=uu, gidx=gidx):
                    if fc == 0:
                        if tc == 0:
                            wtok(e, ffnw_tok["wg0"])
                            wtok(e, ffnw_tok["wu0"])
                            wtok(e, ffnw_tok["wo"])
                            wtok(e, ctok["bo"])
                            e.wait_ge(dv.h, div_dv_all[-1])
                            wtok(e, ocev_dv_all[-1])
                        wtok(e, xnc_tok[tc])
                    if tc == 0:
                        if fc == 2:
                            wtok(e, ffnw_tok["wdA"])
                        elif fc == 5:
                            wtok(e, ffnw_tok["wdB"])
                        elif fc == 8:
                            wtok(e, ffnw_tok["wdC"])
                            wtok(e, ffnw_tok["wu1"])
                        elif fc == 12:
                            wtok(e, ffnw_tok["wdD"])
                            wtok(e, ffnw_tok["wu2"])
                        elif fc == 16:
                            wtok(e, ffnw_tok["wg1"])
                    if gidx >= 2:
                        e.wait_ge(sc.h, silu_sc[gidx - 2])
                        e.wait_ge(dv.h, h_dv[gidx - 2])
                    if gidx == 0 and tc == 0:
                        e.wait_ge(pe.h, bfm_pe)
                    i = None
                    for ic in range(NIC):
                        i = e.matmul(gu[:, 0:256],
                                     wgT_b[:, fc * 1024 + ic * 128: fc * 1024 + ic * 128 + 128],
                                     xnc[tc % 2][:, ic * 256:(ic + 1) * 256],
                                     start=(ic == 0), stop=(ic == NIC - 1))
                    pe.inc(i)
                    gate_pe.append(pe.n)
                    i = None
                    for ic in range(NIC):
                        i = e.matmul(uu,
                                     wuT_b[:, fc * 1024 + ic * 128: fc * 1024 + ic * 128 + 128],
                                     xnc[tc % 2][:, ic * 256:(ic + 1) * 256],
                                     start=(ic == 0), stop=(ic == NIC - 1))
                    pe.inc(i)
                    up_pe.append(pe.n)
                    if fc == 0:
                        wtok(e, ctx_ld_tok[tc])
                        if tc >= 1:
                            e.wait_ge(dv.h, fevac_dv[-1])
                        i = None
                        for th in range(2):
                            for dj in range(2):
                                for ic in range(NIC):
                                    i = e.matmul(
                                        PS[th][:, dj * 512:(dj + 1) * 512],
                                        ctx2[tc % 2][:, th * 1024 + ic * 128:
                                                     th * 1024 + ic * 128 + 128],
                                        wb[:, ic * D + dj * 512: ic * D + dj * 512 + 512],
                                        start=(ic == 0), stop=False)
                                i = e.matmul(
                                    PS[th][:, dj * 512:(dj + 1) * 512],
                                    onesb[0:1, 0:128],
                                    biasb[0:1, BOFF["o"] + dj * 512: BOFF["o"] + dj * 512 + 512],
                                    start=False, stop=False)
                        pe.inc(i)
                        ommf_pe.append(pe.n)
                    if fc >= 2:
                        e.wait_ge(dv.h, h_dv[gidx - 2])
                        i = None
                        for th in range(2):
                            for dj in range(2):
                                i = e.matmul(PS[th][:, dj * 512:(dj + 1) * 512],
                                             hbuf[(fc - 2) % 3][:, th * 128:(th + 1) * 128],
                                             wdT_b[:, WDSLOT[fc - 2] * D + dj * 512:
                                                   WDSLOT[fc - 2] * D + dj * 512 + 512],
                                             start=False, stop=False)
                        pe.inc(i)
                        down_pe.append(pe.n)

                @P.scalar
                def _(e, gu=gu, fc=fc, gidx=gidx):
                    e.wait_ge(pe.h, gate_pe[gidx])
                    if gidx >= 2:
                        e.wait_ge(dv.h, h_dv[gidx - 2])
                    i = e.activation(sg[fc % 2][:], gu[:, 0:256], AF.Silu, scale=WSCALE)
                    sc.inc(i)
                    silu_sc.append(sc.n)

                @P.vector
                def _(e, uu=uu, fc=fc, gidx=gidx):
                    e.wait_ge(pe.h, up_pe[gidx])
                    e.wait_ge(sc.h, silu_sc[gidx])
                    if len(down_pe) >= 1 and gidx >= 3:
                        e.wait_ge(pe.h, down_pe[-1])
                    i = e.scalar_tensor_tensor(hbuf[fc % 3][:], sg[fc % 2][:], W2,
                                               uu, ALU.mult, ALU.mult)
                    dv.inc(i)
                    h_dv.append(dv.n)

            @P.tensor
            def _(e, tc=tc):
                e.wait_ge(dv.h, h_dv[-1])
                i = None
                for fcl in (NFC - 2, NFC - 1):
                    for th in range(2):
                        for dj in range(2):
                            i = e.matmul(PS[th][:, dj * 512:(dj + 1) * 512],
                                         hbuf[fcl % 3][:, th * 128:(th + 1) * 128],
                                         wdT_b[:, WDSLOT[fcl] * D + dj * 512:
                                               WDSLOT[fcl] * D + dj * 512 + 512],
                                         start=False, stop=(fcl == NFC - 1))
                pe.inc(i)
                down_pe.append(pe.n)

            for th in range(2):
                @P.vector
                def _(e, tc=tc, th=th):
                    e.wait_ge(pe.h, down_pe[-1])
                    wtok(e, xres_tok[2 * tc + th])
                    if tc >= 1:
                        wtok(e, out_tok[2 * (tc - 1) + th])
                    i = e.tensor_add(obuf[th][:], PS[th][:], xres2[(2 * tc + th) % 2][:])
                    dv.inc(i)
                    fevac_dv.append(dv.n)

                @P.gpsimd
                def _(e, tc=tc, th=th):
                    e.wait_ge(dv.h, fevac_dv[-1])
                    row = tc * 256 + th * 128
                    out_tok.append(stq.dma(e, out_d[row:row + 128, :],
                                           obuf[th][:]))

        _emit_taps(P, tap_d, taps, locals(), dv, sc, pe, stq)

        @P.gpsimd
        def _(e):
            for sem, cnt in stq.all_tokens():
                e.wait_ge(sem, cnt)
    return nc


def _emit_taps(P, tap_d, taps, env, dv, sc, pe, stq):
    if not taps:
        return

    @P.gpsimd
    def _(e):
        e.wait_ge(dv.h, dv.n)
        e.wait_ge(sc.h, sc.n)
        e.wait_ge(pe.h, pe.n)
        for tname, tshape, tdt in taps:
            buf = env[tname]
            stq.dma(e, tap_d[tname], buf[:])
        for sem, cnt in stq.all_tokens():
            e.wait_ge(sem, cnt)



def _make_consts():
    idt = np.eye(128, dtype=ml_dtypes.bfloat16)
    p = np.zeros((128, 128), dtype=np.float32)
    for b in (0, 64):
        for r in range(32):
            p[b + r, b + 32 + r] = -1.0
            p[b + 32 + r, b + r] = 1.0
    ptm = np.ascontiguousarray(p.T).astype(ml_dtypes.bfloat16)
    rows = np.arange(128) % 64
    freqs = rows % 32
    inv = 1.0 / 10000.0 ** (freqs / 32.0)
    t = np.arange(T, dtype=np.float64)
    ang = inv[:, None] * t[None, :]
    cosb = np.cos(ang).astype(np.float32).astype(ml_dtypes.bfloat16)
    sinb = np.sin(ang).astype(np.float32).astype(ml_dtypes.bfloat16)
    ones = np.ones((1, 128), dtype=ml_dtypes.bfloat16)
    return idt, ptm, cosb, sinb, ones


def _layout_ffn(wg, wu, wd):
    def gu(w):
        a = np.asarray(w, dtype=np.int32).T
        a = a.reshape(NIC, 128, NFC, 128)
        a = a.transpose(1, 2, 0, 3)
        return np.ascontiguousarray(a.reshape(128, NFC * NIC * 128))
    d = np.asarray(wd, dtype=np.int32).T
    d = d.reshape(NFC, 128, D).transpose(1, 0, 2)
    return gu(wg), gu(wu), np.ascontiguousarray(d.reshape(128, NFC * D))


_NC_CACHE = {}


def kernel(L0_attn_x, L0_attn_Wq, L0_attn_bq, L0_attn_Wk, L0_attn_bk,
           L0_attn_Wv, L0_attn_bv, L0_attn_Wo, L0_attn_bo,
           L0_ffn_w_gate, L0_ffn_w_up, L0_ffn_w_down):
    x = np.ascontiguousarray(np.asarray(L0_attn_x, dtype=np.float32))
    B = x.shape[0]
    idt, ptm, cosb, sinb, ones = _make_consts()
    common = {
        "wq": np.ascontiguousarray(np.asarray(L0_attn_Wq, dtype=np.float32)),
        "wk": np.ascontiguousarray(np.asarray(L0_attn_Wk, dtype=np.float32)),
        "wv": np.ascontiguousarray(np.asarray(L0_attn_Wv, dtype=np.float32)),
        "wo": np.ascontiguousarray(np.asarray(L0_attn_Wo, dtype=np.float32)),
        "bq": np.asarray(L0_attn_bq, dtype=np.float32).reshape(1, D),
        "bk": np.asarray(L0_attn_bk, dtype=np.float32).reshape(1, D),
        "bv": np.asarray(L0_attn_bv, dtype=np.float32).reshape(1, D),
        "bo": np.asarray(L0_attn_bo, dtype=np.float32).reshape(1, D),
        "idt": None,
    }
    common["wgT"], common["wuT"], common["wdT"] = _layout_ffn(
        L0_ffn_w_gate, L0_ffn_w_up, L0_ffn_w_down)
    common.update({
        "idt": idt, "pt": ptm, "cosb": cosb, "sinb": sinb, "onesb": ones,
    })
    nseq = B // NCORES
    key = ("nc", nseq)
    if key not in _NC_CACHE:
        _NC_CACHE[key] = build_core(n_seq=nseq)
    nc = _NC_CACHE[key]
    in_maps = []
    for c in range(NCORES):
        m = dict(common)
        m["x"] = np.ascontiguousarray(x[c * nseq:(c + 1) * nseq].reshape(nseq * T, D))
        in_maps.append(m)
    res = run_bass_kernel_spmd(nc, in_maps, core_ids=list(range(NCORES)))
    out = np.concatenate([r["out"].reshape(nseq, T, D) for r in res.results], axis=0)
    return out.astype(np.float32)


# version 48
# speedup vs baseline: 1.0040x; 1.0040x over previous
import numpy as np
import ml_dtypes
from contextlib import ExitStack

import concourse.bass as bass
import concourse.mybir as mybir
from concourse.bass_utils import run_bass_kernel_spmd

F32 = mybir.dt.float32
BF16 = mybir.dt.bfloat16
I32 = mybir.dt.int32
AF = mybir.ActivationFunctionType
ALU = mybir.AluOpType
AX = mybir.AxisListType

T = 2048
D = 1024
NHP = 8
FF = 3072
NFC = FF // 128
NIC = D // 128
NKT = T // 128
NQC = T // 512
QSCALE = 0.125
WSCALE = 1.0 / 127.0
W2 = (1.0 / 127.0) ** 2
EPS = 1e-5
NCORES = 8

WDSLOT = {0: 21, 1: 22, 2: 23, 3: 14, 4: 15, 5: 16, 6: 17, 7: 18, 8: 19, 9: 20}
for _f in range(10, NFC):
    WDSLOT[_f] = _f - 10
del _f


class Sem:
    def __init__(self, E, nc, name):
        self.h = E(nc.semaphore(name))
        self.n = 0

    def inc(self, instr, k=1):
        instr.then_inc(self.h, k)
        self.n += k
        return self.n


class DmaQ:

    def __init__(self, E, nc, name, n=3):
        self.sems = [E(nc.semaphore(f"{name}{i}")) for i in range(n)]
        self.cnt = [0] * n
        self.i = 0

    def dma(self, e, out, in_, **kw):
        j = self.i % len(self.sems)
        self.i += 1
        if self.cnt[j]:
            e.wait_ge(self.sems[j], self.cnt[j])
        e.dma_start(out=out, in_=in_, **kw).then_inc(self.sems[j], 16)
        self.cnt[j] += 16
        return (self.sems[j], self.cnt[j])

    def all_tokens(self):
        return [(self.sems[j], self.cnt[j])
                for j in range(len(self.sems)) if self.cnt[j]]


def wtok(e, tok):
    e.wait_ge(tok[0], tok[1])


class Prog:

    _ET = {"sync": mybir.EngineType.SP, "scalar": mybir.EngineType.Activation,
           "vector": mybir.EngineType.DVE, "tensor": mybir.EngineType.PE,
           "gpsimd": mybir.EngineType.Pool}

    def __init__(self, nc, block):
        self.nc = nc
        self.block = block
        self.names = {}

    def _deco(self, engname):
        def deco(f):
            nc = self.nc
            if engname not in self.names:
                getattr(self.block, engname)(f)
                self.names[engname] = nc.m.functions[0].blocks[-1].name
            else:
                with nc.body(self.names[engname],
                             valid_engines=list(self._ET.values()),
                             allow_existing_parent=True):
                    f(getattr(nc, engname))
        return deco

    @property
    def sync(self):
        return self._deco("sync")

    @property
    def scalar(self):
        return self._deco("scalar")

    @property
    def vector(self):
        return self._deco("vector")

    @property
    def tensor(self):
        return self._deco("tensor")

    @property
    def gpsimd(self):
        return self._deco("gpsimd")


def build_core(n_seq=2, taps=(), stop_after=None):
    nc = bass.Bass("TRN2", target_bir_lowering=False)
    NSEQ = n_seq

    x_d = nc.dram_tensor("x", [NSEQ * T, D], F32, kind="ExternalInput").ap()
    w_d = {w: nc.dram_tensor("w" + w, [D, D], F32, kind="ExternalInput").ap()
           for w in ("q", "k", "v", "o")}
    bias_d = {w: nc.dram_tensor("b" + w, [1, D], F32, kind="ExternalInput").ap()
              for w in ("q", "k", "v", "o")}
    wgT_d = nc.dram_tensor("wgT", [128, NIC * FF], I32, kind="ExternalInput").ap()
    wuT_d = nc.dram_tensor("wuT", [128, NIC * FF], I32, kind="ExternalInput").ap()
    wdT_d = nc.dram_tensor("wdT", [128, NFC * D], I32, kind="ExternalInput").ap()
    idt_d = nc.dram_tensor("idt", [128, 128], BF16, kind="ExternalInput").ap()
    pt_d = nc.dram_tensor("pt", [128, 128], BF16, kind="ExternalInput").ap()
    cos_d = nc.dram_tensor("cosb", [128, T], BF16, kind="ExternalInput").ap()
    sin_d = nc.dram_tensor("sinb", [128, T], BF16, kind="ExternalInput").ap()
    ones_d = nc.dram_tensor("onesb", [1, 128], BF16, kind="ExternalInput").ap()
    out_d = nc.dram_tensor("out", [NSEQ * T, D], F32, kind="ExternalOutput").ap()

    xnscr_d = nc.dram_tensor("xnscr", [128, NSEQ * NIC * T], BF16, kind="Internal").ap()
    ctxscr_d = nc.dram_tensor("ctxscr", [128, NSEQ * NKT * 1024], BF16, kind="Internal").ap()

    tap_d = {}
    for tname, tshape, tdt in taps:
        tap_d[tname] = nc.dram_tensor("tap_" + tname, list(tshape), tdt,
                                      kind="ExternalOutput").ap()

    ctx = ExitStack()
    E = ctx.enter_context
    with ctx:
        idt = E(nc.sbuf_tensor("idt_s", [128, 128], BF16))
        pt = E(nc.sbuf_tensor("pt_s", [128, 128], BF16))
        cosb = E(nc.sbuf_tensor("cos_s", [128, T], BF16))
        sinb = E(nc.sbuf_tensor("sin_s", [128, T], BF16))
        onesb = E(nc.sbuf_tensor("ones_s", [1, 128], BF16))
        biasb = E(nc.sbuf_tensor("bias_s", [1, 4 * D], BF16))
        BOFF = {"q": 0, "k": D, "v": 2 * D, "o": 3 * D}
        zr = E(nc.sbuf_tensor("zr", [128, 16], F32))
        bfm = E(nc.sbuf_tensor("bfm", [128, 16], F32))
        statsA = E(nc.sbuf_tensor("statsA", [128, 32], F32))
        statsB = E(nc.sbuf_tensor("statsB", [128, 48], F32))

        PS = [E(nc.psum_tensor(f"PS{i}", [128, 1024], F32)) for i in range(3)]
        PC = [E(nc.psum_tensor(f"PC{i}", [128, 512], F32)) for i in range(2)]
        pOO = PS[0][:, 0:512]

        dv = Sem(E, nc, "dv_p")
        sc = Sem(E, nc, "sc_p")
        pe = Sem(E, nc, "pe_p")
        spq = DmaQ(E, nc, "spq", 3)
        gpq = DmaQ(E, nc, "gpq", 3)
        stq = DmaQ(E, nc, "stq", 4)

        block = E(nc.Block())
        P = Prog(nc, block)

        actx = ExitStack()
        A = actx.enter_context
        wa = A(nc.sbuf_tensor("wa", [128, NIC * D], BF16))
        wb = A(nc.sbuf_tensor("wb", [128, NIC * D], BF16))
        xn_fm = A(nc.sbuf_tensor("xn_fm", [128, NIC * T], BF16))
        k_fm = A(nc.sbuf_tensor("k_fm", [128, NHP * T], BF16))
        q_fm = A(nc.sbuf_tensor("q_fm", [128, NHP * 512], BF16))
        vp = A(nc.sbuf_tensor("vp", [128, NKT * 1040], BF16))
        exa = A(nc.sbuf_tensor("exa", [128, 3072], BF16))
        ctxtm = A(nc.sbuf_tensor("ctxtm", [128, 4 * 1024], BF16))
        ctxfm = [A(nc.sbuf_tensor(f"ctxfm{i}", [128, 1024], BF16)) for i in range(2)]
        xldr = [A(nc.sbuf_tensor(f"xldr{i}", [128, D], F32)) for i in range(2)]
        kraw = [A(nc.sbuf_tensor(f"kraw{i}", [128, 512], BF16)) for i in range(2)]
        tmps = [A(nc.sbuf_tensor(f"tmps{i}", [128, 512], BF16)) for i in range(2)]
        tmpc = [A(nc.sbuf_tensor(f"tmpc{i}", [128, 512], BF16)) for i in range(2)]
        spacer2 = A(nc.sbuf_tensor("spacer2", [128, 3840], BF16))
        obuf = [A(nc.sbuf_tensor(f"obuf{i}", [128, D], F32)) for i in range(2)]
        sg = [A(nc.sbuf_tensor(f"sg{i}", [128, 256], BF16)) for i in range(2)]
        hbuf = [A(nc.sbuf_tensor(f"hbuf{i}", [128, 256], BF16)) for i in range(2)]
        hbuf.append(nc.alloc_sbuf_tensor_at("hbuf2", [128, 256], BF16,
                                            offset=nc.lookup_mloc(biasb).addr))
        exa2 = A(nc.sbuf_tensor("exa2", [128, 1024], BF16))
        exa3 = A(nc.sbuf_tensor("exa3", [128, 1024], BF16))

        def exslot(j, lo, n):
            j = j % 5
            if j < 3:
                return exa[:, j * 1024 + lo: j * 1024 + lo + n]
            return (exa2, exa3)[j - 3][:, lo: lo + n]
        xb = [ctxtm[:, i * D:(i + 1) * D] for i in range(4)]
        sqscr = ctxfm[0]

        def _addr(h):
            return nc.lookup_mloc(h).addr

        assert _addr(k_fm) == _addr(xn_fm) + 32768
        assert _addr(q_fm) == _addr(k_fm) + 32768
        assert _addr(vp) == _addr(q_fm) + 8192
        assert _addr(spacer2) + 7680 == _addr(vp) + 24576 + 49152, (_addr(spacer2), _addr(vp))
        wgT_b = nc.alloc_sbuf_tensor_at("wgT_b", [128, NIC * FF], BF16,
                                        offset=_addr(xn_fm))
        wuT_b = nc.alloc_sbuf_tensor_at("wuT_b", [128, NIC * FF], BF16,
                                        offset=_addr(k_fm) + 16384)
        wdT_b = nc.alloc_sbuf_tensor_at("wdT_b", [128, NFC * D], BF16,
                                        offset=_addr(vp) + 24576)
        xnc = [nc.alloc_sbuf_tensor_at(f"xnc{i}", [128, NIC * 256], BF16,
                                       offset=_addr(wa) + i * 4096) for i in range(2)]
        ctx2 = [nc.alloc_sbuf_tensor_at(f"ctx2_{i}", [128, 2 * 1024], BF16,
                                        offset=_addr(wa) + 8192 + i * 4096) for i in range(2)]
        xres2 = [nc.alloc_sbuf_tensor_at(f"xres2_{i}", [128, D], F32,
                                         offset=_addr((cosb, sinb)[i]))
                 for i in range(2)]
        ffnw_tok = {}

        ctok = {}

        @P.sync
        def _(e):
            for nm, dst, src in (("idt", idt, idt_d), ("pt", pt, pt_d),
                                 ("ones", onesb, ones_d)):
                ctok[nm] = spq.dma(e, dst[:], src)

        @P.gpsimd
        def _(e):
            for w in ("q", "k", "v", "o"):
                ctok["b" + w] = gpq.dma(e, biasb[:, BOFF[w]:BOFF[w] + D], bias_d[w])
            ffnw_tok["wdA"] = gpq.dma(e, wdT_b[:, 21 * D:24 * D], wdT_d[:, 0:3 * D])

        wload_tok = {}

        def load_w(buf, name, key, wait_pe=0, split=None):
            @P.gpsimd
            def _(e):
                if wait_pe:
                    e.wait_ge(pe.h, wait_pe)
                dst = buf[:].rearrange("p (ic o) -> p ic o", ic=NIC)
                src = w_d[key].rearrange("(ic p) o -> p ic o", p=128)
                if split is None:
                    wload_tok[name] = gpq.dma(e, dst, src)
                else:
                    lo = 0
                    for ci, hi in enumerate(split):
                        wload_tok[(name, ci)] = gpq.dma(
                            e, dst[:, :, lo:hi], src[:, :, lo:hi])
                        lo = hi

        load_w(wa, "wk0", "k")
        load_w(wb, "wv0", "v")

        @P.tensor
        def _(e):
            wtok(e, ctok["bq"])
            wtok(e, ctok["bk"])
            wtok(e, ctok["ones"])
            i = None
            for wi, w in enumerate(("q", "k")):
                for oc in range(NHP):
                    i = e.matmul(PS[2][:, wi * 8 + oc: wi * 8 + oc + 1],
                                 biasb[0:1, BOFF[w] + oc * 128: BOFF[w] + oc * 128 + 128],
                                 onesb[0:1, 0:1], start=True, stop=True)
            pe.inc(i)
        bfm_pe = pe.n

        @P.vector
        def _(e):
            e.wait_ge(pe.h, bfm_pe)
            i = e.tensor_copy(bfm[:], PS[2][:, 0:16])
            dv.inc(i)
        bfm_dv = dv.n

        NTCH = NSEQ * T // 256
        xnc_tok = []
        ctx_ld_tok = []
        xres_tok = []

        def _ffn_act_loads(e, u):
            s_idx = (u * 256) // T
            w_idx = (u * 256) % T
            src = xnscr_d[:, s_idx * NIC * T: (s_idx + 1) * NIC * T]\
                .rearrange("p (ic t) -> p ic t", ic=NIC)[:, :, w_idx:w_idx + 256]
            xnc_tok.append(spq.dma(
                e, xnc[u % 2][:].rearrange("p (ic t) -> p ic t", ic=NIC), src))
            wtok(e, ctx_tok[2 * u])
            wtok(e, ctx_tok[2 * u + 1])
            ctx_ld_tok.append(spq.dma(
                e, ctx2[u % 2][:], ctxscr_d[:, u * 2048:(u + 1) * 2048]))

        exp_sc_all = []
        sco_pe_all = []
        pv_pe_all = []
        div_dv_all = []
        ocev_dv_all = []
        otp_pe_all = []
        ctx_tok = {}
        ln_sq_sc = []
        state = {"last_rope_pe": 0, "xn_read_pe": 0, "tpc_pe_last": 0,
                 "stats_sc": 0}

        for s in range(NSEQ):
            ln_x_tok = []
            ln_sq_sc_l = []
            ln_red_dv = []
            ln_nmr_dv = []
            ln_tp_pe = []
            ln_evac_dv = []
            ln_x2_tok = []

            lnring = [xldr[0], xldr[1], obuf[0], obuf[1]]
            for tt in range(NKT):
                par = tt % 4

                @P.sync
                def _(e, tt=tt, par=par, s=s):
                    if s == 0 and tt == 4 and "cos" not in ctok:
                        ctok["cos"] = spq.dma(e, cosb[:], cos_d)
                        ctok["sin"] = spq.dma(e, sinb[:], sin_d)
                    if s > 0 and tt < 4:
                        e.wait_ge(sc.h, state["xldr_sc"])
                        e.wait_ge(dv.h, state["xldr_dv"])
                    if tt >= 4:
                        e.wait_ge(sc.h, ln_sq_sc_l[tt - 4])
                        e.wait_ge(dv.h, ln_red_dv[tt - 4])
                    ln_x_tok.append(spq.dma(
                        e, lnring[par][:],
                        x_d[s * T + tt * 128: s * T + tt * 128 + 128, :]))

                @P.scalar
                def _(e, tt=tt, par=par):
                    wtok(e, ln_x_tok[tt])
                    if tt == 0:
                        if ocev_dv_all:
                            wtok(e, ocev_dv_all[-1])
                        if ctx_tok:
                            for qtg in sorted(ctx_tok)[-2:]:
                                wtok(e, ctx_tok[qtg])
                    else:
                        e.wait_ge(sc.h, ln_sq_sc_l[tt - 1])
                    i = e.activation(sqscr[:], lnring[par][:], AF.Square,
                                     accum_out=statsA[:, 16 + tt:17 + tt])
                    sc.inc(i)
                    ln_sq_sc.append(sc.n)
                    ln_sq_sc_l.append(sc.n)

                @P.vector
                def _(e, tt=tt, par=par):
                    wtok(e, ln_x_tok[tt])
                    if tt == 0 and s > 0:
                        e.wait_ge(sc.h, state["stats_sc"])
                    i = e.reduce_sum(statsA[:, tt:tt + 1], lnring[par][:], axis=AX.X)
                    dv.inc(i)
                    ln_red_dv.append(dv.n)

            @P.vector
            def _(e):
                e.wait_ge(dv.h, ln_red_dv[-1])
                i = e.tensor_scalar_mul(statsB[:, 0:16], statsA[:, 0:16], 1.0 / D)
                dv.inc(i)
                e.wait_ge(dv.h, dv.n)
                i = e.tensor_mul(statsB[:, 16:32], statsB[:, 0:16], statsB[:, 0:16])
                dv.inc(i)
                e.wait_ge(sc.h, ln_sq_sc_l[-1])
                e.wait_ge(dv.h, dv.n)
                i = e.scalar_tensor_tensor(statsA[:, 0:16], statsA[:, 16:32], 1.0 / D,
                                           statsB[:, 16:32], ALU.mult, ALU.subtract)
                dv.inc(i)
                e.wait_ge(dv.h, dv.n)
                i = e.tensor_scalar_add(statsB[:, 16:32], statsA[:, 0:16], EPS)
                dv.inc(i)
            var_dv = dv.n

            @P.scalar
            def _(e, var_dv=var_dv):
                e.wait_ge(dv.h, var_dv)
                i = e.activation(statsA[:, 0:16], statsB[:, 16:32], AF.Ln)
                sc.inc(i)
                e.wait_ge(sc.h, sc.n)
                i = e.activation(statsB[:, 32:48], statsA[:, 0:16], AF.Exp, scale=-0.5)
                sc.inc(i)
            rstd_sc = sc.n
            state["stats_sc"] = rstd_sc

            for tt in range(NKT):
                par = tt % 4

                @P.sync
                def _(e, tt=tt, par=par, s=s):
                    if tt < 4:
                        e.wait_ge(sc.h, ln_sq_sc_l[12 + par])
                        e.wait_ge(dv.h, ln_red_dv[12 + par])
                    else:
                        e.wait_ge(dv.h, ln_nmr_dv[tt - 4])
                    ln_x2_tok.append(spq.dma(
                        e, lnring[par][:],
                        x_d[s * T + tt * 128: s * T + tt * 128 + 128, :]))

                @P.vector
                def _(e, tt=tt, par=par, rstd_sc=rstd_sc):
                    wtok(e, ln_x2_tok[tt])
                    if tt < 4:
                        e.wait_ge(sc.h, rstd_sc)
                        e.wait_ge(pe.h, state["tpc_pe_last"])
                        for qtg in sorted(ctx_tok)[-4:]:
                            wtok(e, ctx_tok[qtg])
                    else:
                        e.wait_ge(pe.h, ln_tp_pe[tt - 4])
                    i = e.tensor_scalar(xb[par][:], lnring[par][:],
                                        statsB[:, tt:tt + 1], statsB[:, 32 + tt:33 + tt],
                                        ALU.subtract, ALU.mult)
                    dv.inc(i)
                    ln_nmr_dv.append(dv.n)

                @P.tensor
                def _(e, tt=tt, par=par, s=s):
                    e.wait_ge(dv.h, ln_nmr_dv[tt])
                    if tt >= 1:
                        wtok(e, ln_evac_dv[tt - 1])
                    else:
                        if ocev_dv_all:
                            wtok(e, ocev_dv_all[-1])
                        if s == 0:
                            wtok(e, ctok["idt"])
                            e.wait_ge(dv.h, bfm_dv)
                    pst = PS[2]
                    i = None
                    for ic in range(NIC):
                        i = e.matmul(pst[:, ic * 128:(ic + 1) * 128],
                                     xb[par][:, ic * 128:(ic + 1) * 128], idt[:],
                                     start=True, stop=True)
                    pe.inc(i)
                    ln_tp_pe.append(pe.n)

                @P.scalar
                def _(e, tt=tt, s=s):
                    e.wait_ge(pe.h, ln_tp_pe[tt])
                    if tt == 0 and s > 0:
                        e.wait_ge(pe.h, state["xn_read_pe"])
                        wtok(e, state[("xnscr_tok", s - 1)])
                    pst = PS[2]
                    dst = xn_fm[:].rearrange("p (ic t) -> p ic t", ic=NIC)[:, :, tt * 128:tt * 128 + 128]
                    src = pst.rearrange("p (ic c) -> p ic c", ic=NIC)
                    i = e.copy(dst, src)
                    sc.inc(i)
                    ln_evac_dv.append((sc.h, sc.n))

            xn_tok = ln_evac_dv[-1]
            state["xldr_sc"] = ln_sq_sc_l[-1]
            state["xldr_dv"] = ln_nmr_dv[-1]

            @P.sync
            def _(e, s=s, xn_tok=xn_tok):
                wtok(e, xn_tok)
                state[("xnscr_tok", s)] = spq.dma(
                    e, xnscr_d[:, s * NIC * T:(s + 1) * NIC * T], xn_fm[:])

            if s == NSEQ - 1 and stop_after is None:
                @P.gpsimd
                def _(e, xsc=state["xldr_sc"], xdv=state["xldr_dv"]):
                    e.wait_ge(sc.h, xsc)
                    e.wait_ge(dv.h, xdv)
                    ffnw_tok["wdB"] = gpq.dma(
                        e, wdT_b[:, 14 * D:17 * D], wdT_d[:, 3 * D:6 * D])

            if stop_after == "ln" and s == NSEQ - 1:
                break

            kv_proj_pe = []
            kv_tmp_dv = []
            kv_rope_pe = []
            kv_evac_dv = []
            v_proj_pe = []
            v_evac_dv = []
            kraw_sc = []
            punit_free = []

            @P.vector
            def _(e, s=s):
                if ("vp_read_pe", s - 1) in state:
                    e.wait_ge(pe.h, state["vp_read_pe", s - 1])
                i = e.memset(vp[:].rearrange("p (kt h c) -> p kt h c",
                                             kt=NKT, h=16)[:, :, :, 64:65], 1.0)
                dv.inc(i)
            vones_dv = dv.n

            def emit_k_rope(jt, s=s):
                par = jt % 2
                pR = PS[1][:, par * 512:(par + 1) * 512]

                @P.tensor
                def _(e, par=par, pR=pR, jt=jt, s=s):
                    e.wait_ge(dv.h, kv_tmp_dv[jt])
                    if jt >= 2:
                        wtok(e, kv_evac_dv[jt - 2])
                    if jt == 0 and s == 0:
                        wtok(e, ctok["pt"])
                    e.matmul(pR, idt[:], tmpc[par][:], start=True, stop=False)
                    i = e.matmul(pR, pt[:], tmps[par][:], start=False, stop=True)
                    pe.inc(i)
                    kv_rope_pe.append(pe.n)

                tc, hp = jt // NHP, jt % NHP

                @P.scalar
                def _(e, hp=hp, tc=tc, pR=pR, jt=jt, s=s):
                    e.wait_ge(pe.h, kv_rope_pe[jt])
                    if s > 0 and jt == 0:
                        e.wait_ge(pe.h, state["kfm_read_pe", s - 1])
                    i = e.copy(k_fm[:, hp * T + tc * 512: hp * T + tc * 512 + 512], pR)
                    sc.inc(i)
                    kv_evac_dv.append((sc.h, sc.n))

            uidx = 0
            kidx = 0
            vidx = 0
            for tc in range(4):
                for u in range(16):
                    par = uidx % 2
                    pP = PS[0][:, par * 512:(par + 1) * 512]
                    is_k = (u % 2 == 1)
                    if is_k:
                        hp = u // 2
                        it = kidx

                        @P.tensor
                        def _(e, hp=hp, tc=tc, pP=pP, it=it, uidx=uidx, s=s,
                              lev=list(ln_evac_dv)):
                            if it == 0:
                                wtok(e, wload_tok[f"wk{s}"])
                                wtok(e, ctok["bk"])
                                if s == 0:
                                    wtok(e, ctok["ones"])
                            wtok(e, lev[4 * tc + 3])
                            if uidx >= 2:
                                wtok(e, punit_free[uidx - 2])
                            i = None
                            for ic in range(NIC):
                                i = e.matmul(pP, wa[:, ic * D + hp * 128: ic * D + hp * 128 + 128],
                                             xn_fm[:, ic * T + tc * 512: ic * T + tc * 512 + 512],
                                             start=(ic == 0), stop=(ic == NIC - 1))
                            pe.inc(i)
                            kv_proj_pe.append(pe.n)

                        @P.scalar
                        def _(e, hp=hp, par=par, pP=pP, it=it, s=s):
                            e.wait_ge(pe.h, kv_proj_pe[it])
                            if it >= 2:
                                e.wait_ge(dv.h, kv_tmp_dv[it - 2])
                            if it == 0 and s == 0:
                                e.wait_ge(dv.h, bfm_dv)
                            i = e.activation(kraw[it % 2][:], pP, AF.Identity,
                                             bias=bfm[:, 8 + hp:9 + hp])
                            sc.inc(i)
                            kraw_sc.append(sc.n)
                            punit_free.append((sc.h, sc.n))

                        @P.vector
                        def _(e, tc=tc, it=it, s=s):
                            e.wait_ge(sc.h, kraw_sc[it])
                            if it >= 2:
                                e.wait_ge(pe.h, kv_rope_pe[it - 2])
                            elif state["last_rope_pe"]:
                                e.wait_ge(pe.h, state["last_rope_pe"])
                            if it == 0 and s == 0:
                                wtok(e, ctok["cos"])
                                wtok(e, ctok["sin"])
                            i = e.tensor_mul(tmps[it % 2][:], kraw[it % 2][:],
                                             sinb[:, tc * 512:(tc + 1) * 512])
                            dv.inc(i)
                            i = e.tensor_mul(tmpc[it % 2][:], kraw[it % 2][:],
                                             cosb[:, tc * 512:(tc + 1) * 512])
                            dv.inc(i)
                            kv_tmp_dv.append(dv.n)

                        if kidx >= 1:
                            emit_k_rope(kidx - 1)
                        kidx += 1
                    else:
                        tt = tc * 4 + u // 4
                        j = (u // 2) % 2
                        it = vidx

                        @P.tensor
                        def _(e, tt=tt, j=j, pP=pP, it=it, uidx=uidx, s=s,
                              lev=list(ln_evac_dv)):
                            if it == 0:
                                wtok(e, wload_tok[f"wv{s}"])
                                wtok(e, ctok["bv"])
                                if s == 0:
                                    wtok(e, ctok["ones"])
                            wtok(e, lev[tt])
                            if uidx >= 2:
                                wtok(e, punit_free[uidx - 2])
                            i = None
                            for ic in range(NIC):
                                i = e.matmul(pP, xn_fm[:, ic * T + tt * 128: ic * T + tt * 128 + 128],
                                             wb[:, ic * D + j * 512: ic * D + j * 512 + 512],
                                             start=(ic == 0), stop=False)
                            i = e.matmul(pP, onesb[0:1, 0:128],
                                         biasb[0:1, BOFF["v"] + j * 512: BOFF["v"] + j * 512 + 512],
                                         start=False, stop=True)
                            pe.inc(i)
                            v_proj_pe.append(pe.n)

                        @P.scalar
                        def _(e, tt=tt, j=j, pP=pP, it=it, vones_dv=vones_dv):
                            e.wait_ge(pe.h, v_proj_pe[it])
                            e.wait_ge(dv.h, vones_dv)
                            dst = vp[:].rearrange("p (kt h c) -> p kt h c",
                                                  kt=NKT, h=16)[:, tt, 8 * j:8 * j + 8, 0:64]
                            src = pP.rearrange("p (h c) -> p h c", h=8)
                            i = e.copy(dst, src)
                            sc.inc(i)
                            v_evac_dv.append((sc.h, sc.n))
                            punit_free.append((sc.h, sc.n))

                        vidx += 1
                    uidx += 1
            emit_k_rope(kidx - 1)
            state["last_rope_pe"] = kv_rope_pe[-1]

            v_done_tok = v_evac_dv[-1]
            load_w(wa, f"wq{s}", "q", wait_pe=kv_proj_pe[-1],
                   split=tuple(range(128, 1025, 128)))
            if s == NSEQ - 1:
                @P.gpsimd
                def _(e):
                    e.wait_ge(pe.h, v_proj_pe[-1])
                    ffnw_tok["wo"] = gpq.dma(
                        e, wb[:].rearrange("p (ic o) -> p ic o", ic=NIC),
                        w_d["o"].rearrange("(ic p) o -> p ic o", p=128))

            if stop_after == "kv" and s == NSEQ - 1:
                break

            state["kv_evac_last"] = kv_evac_dv[-1]
            qp = {}

            def new_qpset(qc_t, prev):
                qp[qc_t] = {"raw": [], "kraw": [], "tmp": [], "rope": [],
                            "evac": [], "prev": prev}

            def emit_qproj_raw(qc_t, h, pc, s=s, sa=False):
                L = qp[qc_t]
                pP2 = (PS[0][:, (h % 2) * 512:(h % 2) * 512 + 512] if sa
                       else PS[2][:, 0:512])
                dep = 2 if sa else 1

                @P.tensor
                def _(e, qc_t=qc_t, h=h, L=L, sa=sa, dep=dep, pP2=pP2, pc=pc):
                    if pc[0] == 0:
                        wtok(e, wload_tok[(f"wq{s}", h)])
                        if h == 0:
                            if sa:
                                wtok(e, v_done_tok)
                            else:
                                if L["prev"]["kraw_dv"]:
                                    e.wait_ge(dv.h, L["prev"]["kraw_dv"])
                                if L["prev"]["ps2_sc"]:
                                    wtok(e, L["prev"]["ps2_sc"])
                        if h >= dep:
                            e.wait_ge(dv.h, L["kraw"][h - dep])
                    i = None
                    for ic in range(pc[0], pc[1]):
                        i = e.matmul(pP2,
                                     wa[:, ic * D + h * 128: ic * D + h * 128 + 128],
                                     xn_fm[:, ic * T + qc_t * 512: ic * T + qc_t * 512 + 512],
                                     start=(ic == 0), stop=(ic == NIC - 1))
                    if pc[1] == NIC:
                        pe.inc(i)
                        L["raw"].append(pe.n)

                @P.vector
                def _(e, qc_t=qc_t, h=h, L=L, pP2=pP2, pc=pc):
                    if pc[1] != NIC:
                        return
                    par = h % 2
                    e.wait_ge(pe.h, L["raw"][h])
                    if h >= 2:
                        e.wait_ge(pe.h, L["rope"][h - 2])
                    elif L["prev"]["rope_pe"]:
                        e.wait_ge(pe.h, L["prev"]["rope_pe"])
                    i = e.scalar_tensor_tensor(tmps[par][:], pP2, bfm[:, h:h + 1],
                                               sinb[:, qc_t * 512:(qc_t + 1) * 512],
                                               ALU.add, ALU.mult)
                    dv.inc(i)
                    i = e.scalar_tensor_tensor(tmpc[par][:], pP2, bfm[:, h:h + 1],
                                               cosb[:, qc_t * 512:(qc_t + 1) * 512],
                                               ALU.add, ALU.mult)
                    dv.inc(i)
                    L["tmp"].append(dv.n)
                    L["kraw"].append(dv.n)

            def emit_qproj_b(qc_t, h, qfm_free, sa=False):
                L = qp[qc_t]
                pR2 = (PS[1][:, (h % 2) * 512:(h % 2) * 512 + 512] if sa
                       else PS[2][:, 512:1024])
                dep = 2 if sa else 1

                @P.tensor
                def _(e, h=h, L=L, sa=sa, dep=dep, pR2=pR2):
                    e.wait_ge(dv.h, L["tmp"][h])
                    if h >= dep:
                        e.wait_ge(dv.h, L["evac"][h - dep])
                    elif sa:
                        wtok(e, kv_evac_dv[-1])
                    elif L["prev"]["evac_dv"]:
                        e.wait_ge(dv.h, L["prev"]["evac_dv"])
                    e.matmul(pR2, idt[:], tmpc[h % 2][:], start=True, stop=False)
                    i = e.matmul(pR2, pt[:], tmps[h % 2][:], start=False, stop=True)
                    pe.inc(i)
                    L["rope"].append(pe.n)

                @P.vector
                def _(e, h=h, L=L, qfm_free=qfm_free, pR2=pR2):
                    e.wait_ge(pe.h, L["rope"][h])
                    if qfm_free:
                        e.wait_ge(pe.h, qfm_free)
                    i = e.tensor_scalar_mul(q_fm[:, h * 512:(h + 1) * 512], pR2, 1.0)
                    dv.inc(i)
                    L["evac"].append(dv.n)

            def emit_ctx_tp(qc_t, qt, ocl, div_tok, s=s, pc=(0, NIC)):
                qtg = (s * NQC + qc_t) * 4 + qt

                @P.tensor
                def _(e, qt=qt, div_tok=div_tok, ocl=ocl, pc=pc):
                    if pc[0] == 0:
                        e.wait_ge(dv.h, div_tok)
                        if qt >= 1:
                            wtok(e, ocl[qt - 1])
                        elif ocev_dv_all:
                            wtok(e, ocev_dv_all[-1])
                        else:
                            wtok(e, ln_evac_dv[-1])
                    i = None
                    for ic in range(pc[0], pc[1]):
                        i = e.matmul(PS[2][:, ic * 128:(ic + 1) * 128],
                                     ctxtm[:, qt * 1024 + ic * 128:
                                           qt * 1024 + ic * 128 + 128],
                                     idt[:], start=True, stop=True)
                    if pc[1] == NIC:
                        pe.inc(i)
                        otp_pe_all.append(pe.n)

                if pc[1] != NIC:
                    return

                @P.vector
                def _(e, qt=qt, qtg=qtg):
                    e.wait_ge(pe.h, otp_pe_all[-1])
                    if (qtg - 4) in ctx_tok:
                        wtok(e, ctx_tok[qtg - 4])
                    i = e.tensor_scalar_mul(ctxtm[:, qt * 1024:(qt + 1) * 1024],
                                            PS[2][:], 1.0)
                    dv.inc(i)
                    tok = (dv.h, dv.n)
                    ocev_dv_all.append(tok)
                    ocl.append(tok)

                @P.sync
                def _(e, qt=qt, qtg=qtg, ocl=ocl):
                    wtok(e, ocl[-1])
                    ctx_tok[qtg] = spq.dma(
                        e, ctxscr_d[:, qtg * 1024:(qtg + 1) * 1024],
                        ctxtm[:, qt * 1024:(qt + 1) * 1024])

                if qt == 3:
                    state["tpc_pe_last"] = otp_pe_all[-1]

            new_qpset(0, {"kraw_dv": 0, "ps2_sc": None,
                          "rope_pe": state["last_rope_pe"], "evac_dv": 0})
            qfm0 = state.get(("kfm_read_pe", s - 1), 0)
            for h in range(NHP):
                emit_qproj_raw(0, h, (0, NIC), sa=True)
                if h >= 1:
                    emit_qproj_b(0, h - 1, qfm0, sa=True)
            emit_qproj_b(0, NHP - 1, qfm0, sa=True)

            for qc in range(NQC):
                tp_ocl = []
                tp_div = div_dv_all[-1] if qc >= 1 else 0
                if s == NSEQ - 1 and qc == NQC - 1 and stop_after is None:
                    @P.gpsimd
                    def _(e, s=s, qpt=qp[NQC - 1]["raw"][-1],
                          qr=qp[NQC - 1]["rope"][-1],
                          xsc=state["xldr_sc"], xdv=state["xldr_dv"]):
                        e.wait_ge(pe.h, qpt)
                        wtok(e, state[("xnscr_tok", s)])
                        ffnw_tok["wg0"] = gpq.dma(
                            e, wgT_b[:, 0:16 * 1024], wgT_d[:, 0:16 * 1024])
                        e.wait_ge(pe.h, qr)
                        e.wait_ge(sc.h, xsc)
                        e.wait_ge(dv.h, xdv)
                        ffnw_tok["wdC"] = gpq.dma(
                            e, wdT_b[:, 17 * D:21 * D], wdT_d[:, 6 * D:10 * D])

                def emit_pv(hp, g, gi):
                    @P.tensor
                    def _(e, hp=hp, g=g, gi=gi):
                        e.wait_ge(sc.h, exp_sc_all[gi])
                        if g == 0 and div_dv_all:
                            e.wait_ge(dv.h, div_dv_all[-1])
                        i = None
                        for h2 in range(2):
                            for qt in range(4):
                                i = e.matmul(
                                    PC[qt // 2][:, (qt % 2) * 130 + h2 * 65:
                                                (qt % 2) * 130 + h2 * 65 + 65],
                                    exslot(gi, h2 * 512 + qt * 128, 128),
                                    vp[:, g * 1040 + (2 * hp + h2) * 65:
                                       g * 1040 + (2 * hp + h2) * 65 + 65],
                                    start=(g == 0 and h2 == 0 and qt % 2 == 0),
                                    stop=(g == NKT - 1 and h2 == 1 and qt % 2 == 1))
                        pe.inc(i)
                        pv_pe_all.append(pe.n)

                def emit_divide(hp_t):
                    @P.vector
                    def _(e, hp=hp_t):
                        e.wait_ge(pe.h, pv_pe_all[-1])
                        if hp == 0:
                            e.wait_ge(pe.h, state["tpc_pe_last"])
                            for qtg in sorted(ctx_tok)[-4:]:
                                wtok(e, ctx_tok[qtg])
                        for i2 in range(2):
                            src = PC[i2][:, 0:260].rearrange("p (q2 h2 c) -> p q2 h2 c",
                                                             q2=2, h2=2)[:, :, :, 64:65]
                            dst = zr[:, 4 * i2:4 * i2 + 4].rearrange("p (a b c) -> p a b c",
                                                                     a=2, b=2)
                            i = e.tensor_copy(dst, src)
                            dv.inc(i)
                        e.wait_ge(dv.h, dv.n)
                        i = e.reciprocal(zr[:, 8:16], zr[:, 0:8])
                        dv.inc(i)
                        e.wait_ge(dv.h, dv.n)
                        for qt in range(4):
                            for h2 in range(2):
                                zcol = 8 + (qt // 2) * 4 + (qt % 2) * 2 + h2
                                i = e.tensor_scalar(
                                    ctxtm[:, qt * 1024 + (2 * hp + h2) * 64:
                                          qt * 1024 + (2 * hp + h2) * 64 + 64],
                                    PC[qt // 2][:, (qt % 2) * 130 + h2 * 65:
                                                (qt % 2) * 130 + h2 * 65 + 64],
                                    zr[:, zcol:zcol + 1], None, ALU.mult)
                                dv.inc(i)
                        div_dv_all.append(dv.n)

                for hp in range(NHP):
                    for g in range(NKT):
                        gi = len(sco_pe_all)

                        @P.tensor
                        def _(e, hp=hp, g=g, gi=gi, qc=qc, v_done_tok=v_done_tok):
                            if gi >= 2:
                                e.wait_ge(sc.h, exp_sc_all[gi - 2])
                            if g == 0:
                                e.wait_ge(dv.h, qp[qc]["evac"][hp])
                            if qc == 0 and hp == 0:
                                if g == 0:
                                    e.wait_ge(dv.h, qp[0]["kraw"][-1])
                                elif g == 1:
                                    e.wait_ge(dv.h, qp[0]["evac"][-1])
                            ps_t = PS[gi % 2]
                            e.matmul(ps_t[:, 0:512],
                                     k_fm[0:64, hp * T + g * 128: hp * T + g * 128 + 128],
                                     q_fm[0:64, hp * 512:(hp + 1) * 512],
                                     start=True, stop=True, tile_position=(0, 0))
                            i = e.matmul(ps_t[:, 512:1024],
                                         k_fm[64:128, hp * T + g * 128: hp * T + g * 128 + 128],
                                         q_fm[64:128, hp * 512:(hp + 1) * 512],
                                         start=True, stop=True, tile_position=(64, 0))
                            pe.inc(i)
                            sco_pe_all.append(pe.n)

                        @P.scalar
                        def _(e, g=g, gi=gi):
                            e.wait_ge(pe.h, sco_pe_all[gi])
                            if gi >= 5:
                                e.wait_ge(pe.h, pv_pe_all[gi - 5])
                            i = e.activation(exslot(gi, 0, 1024),
                                             PS[gi % 2][:], AF.Exp, scale=QSCALE)
                            sc.inc(i)
                            exp_sc_all.append(sc.n)

                        if g == 0 and hp >= 1:
                            emit_pv(hp - 1, NKT - 4, len(sco_pe_all) - 5)
                            emit_pv(hp - 1, NKT - 3, len(sco_pe_all) - 4)
                            if qc + 1 < NQC:
                                emit_qproj_b(qc + 1, hp - 1, sco_pe_all[-2])
                        elif g == 1 and hp >= 1:
                            emit_pv(hp - 1, NKT - 2, len(sco_pe_all) - 4)
                            emit_pv(hp - 1, NKT - 1, len(sco_pe_all) - 3)
                            emit_divide(hp - 1)
                        elif g >= 4:
                            emit_pv(hp, g - 4, gi - 4)
                        if hp == 0 and qc >= 1 and 2 <= g <= 9:
                            for k in (2 * (g - 2), 2 * (g - 2) + 1):
                                emit_ctx_tp(qc - 1, k // 4, tp_ocl, tp_div,
                                            pc=(2 * (k % 4), 2 * (k % 4) + 2))
                        if 9 <= g <= 12 and qc + 1 < NQC:
                            if hp == 0 and g == 9:
                                if qc == 0:
                                    new_qpset(1, {
                                        "kraw_dv": 0,
                                        "ps2_sc": ln_evac_dv[-1],
                                        "rope_pe": qp[0]["rope"][-1],
                                        "evac_dv": 0})
                                else:
                                    new_qpset(qc + 1, {
                                        "kraw_dv": qp[qc]["kraw"][-1],
                                        "ps2_sc": ocev_dv_all[-1],
                                        "rope_pe": qp[qc]["rope"][-1],
                                        "evac_dv": qp[qc]["evac"][-1]})
                            emit_qproj_raw(qc + 1, hp, (2 * (g - 9), 2 * (g - 9) + 2))

                for gt in (4, 3, 2, 1):
                    emit_pv(NHP - 1, NKT - gt, len(sco_pe_all) - gt)
                if qc + 1 < NQC:
                    emit_qproj_b(qc + 1, NHP - 1, sco_pe_all[-1])
                emit_divide(NHP - 1)

                state["kfm_read_pe", s] = sco_pe_all[-1]
                state["vp_read_pe", s] = pv_pe_all[-1]
                if qc == NQC - 1:
                    state["xn_read_pe"] = qp[qc]["raw"][-1]
                    state["last_rope_pe"] = qp[qc]["rope"][-1]

                if s == NSEQ - 1 and qc == NQC - 1 and stop_after is None:
                    @P.gpsimd
                    def _(e, sco=sco_pe_all[-1]):
                        e.wait_ge(pe.h, sco)
                        ffnw_tok["wu0"] = gpq.dma(
                            e, wuT_b[:, 0:8 * 1024], wuT_d[:, 0:8 * 1024])
                        ffnw_tok["wu1"] = gpq.dma(
                            e, wuT_b[:, 8 * 1024:12 * 1024], wuT_d[:, 8 * 1024:12 * 1024])
                        ffnw_tok["wg1"] = gpq.dma(
                            e, wgT_b[:, 16 * 1024:24 * 1024], wgT_d[:, 16 * 1024:24 * 1024])

                    @P.sync
                    def _(e, xnr=state["xn_read_pe"], tmp3=qp[NQC - 1]["tmp"][-1]):
                        e.wait_ge(pe.h, xnr)
                        wtok(e, state[("xnscr_tok", 0)])
                        _ffn_act_loads(e, 0)
                        if NTCH > 1:
                            _ffn_act_loads(e, 1)
                        e.wait_ge(dv.h, tmp3)
                        for th in range(2):
                            xres_tok.append(spq.dma(
                                e, xres2[th][:], x_d[th * 128:(th + 1) * 128, :]))

                if qc == NQC - 1:
                    ocl3 = []
                    dt3 = div_dv_all[-1]
                    for qt in range(4):
                        emit_ctx_tp(qc, qt, ocl3, dt3)

            if s + 1 < NSEQ:
                load_w(wa, f"wk{s + 1}", "k", wait_pe=state["xn_read_pe"])
                load_w(wb, f"wv{s + 1}", "v", wait_pe=v_proj_pe[-1])

        if stop_after in ("ln", "kv"):
            _emit_taps(P, tap_d, taps, locals(), dv, sc, pe, stq)
            actx.close()
            return nc

        @P.gpsimd
        def _(e):
            e.wait_ge(pe.h, pv_pe_all[-1])
            e.wait_ge(sc.h, exp_sc_all[-1])
            e.wait_ge(pe.h, otp_pe_all[-1])
            e.wait_ge(dv.h, dv.n)
            for qtg in sorted(ctx_tok)[-4:]:
                wtok(e, ctx_tok[qtg])
            ffnw_tok["wdD"] = gpq.dma(
                e, wdT_b[:, 0:14 * D], wdT_d[:, 10 * D:24 * D])
            ffnw_tok["wu2"] = gpq.dma(
                e, wuT_b[:, 12 * 1024:24 * 1024], wuT_d[:, 12 * 1024:24 * 1024])

        actx.close()

        attn_pe = pe.n
        attn_dv = dv.n
        attn_sc = sc.n
        sc_base_ffn = sc.n

        ommf_pe = []
        gate_pe = []
        up_pe = []
        down_pe = []
        silu_sc = []
        h_dv = []
        fevac_dv = []
        out_tok = []

        for tc in range(NTCH):
            if tc >= 1:
                @P.sync
                def _(e, tc=tc):
                    u = tc + 1
                    if u < NTCH:
                        e.wait_ge(pe.h, up_pe[(tc - 1) * NFC + NFC - 1])
                        e.wait_ge(pe.h, ommf_pe[tc - 1])
                        _ffn_act_loads(e, u)
                    for th in range(2):
                        j = 2 * tc + th
                        e.wait_ge(dv.h, fevac_dv[j - 2])
                        row = tc * 256 + th * 128
                        xres_tok.append(spq.dma(
                            e, xres2[j % 2][:], x_d[row:row + 128, :]))


            for fc in range(NFC):
                gu = PC[fc % 2]
                uu = PS[2][:, (fc % 2) * 512:(fc % 2) * 512 + 256]
                gidx = tc * NFC + fc

                @P.tensor
                def _(e, tc=tc, fc=fc, gu=gu, uu=uu, gidx=gidx):
                    if fc == 0:
                        if tc == 0:
                            wtok(e, ffnw_tok["wg0"])
                            wtok(e, ffnw_tok["wu0"])
                            wtok(e, ffnw_tok["wo"])
                            wtok(e, ctok["bo"])
                            e.wait_ge(dv.h, div_dv_all[-1])
                            wtok(e, ocev_dv_all[-1])
                        wtok(e, xnc_tok[tc])
                    if tc == 0:
                        if fc == 2:
                            wtok(e, ffnw_tok["wdA"])
                        elif fc == 5:
                            wtok(e, ffnw_tok["wdB"])
                        elif fc == 8:
                            wtok(e, ffnw_tok["wdC"])
                            wtok(e, ffnw_tok["wu1"])
                        elif fc == 12:
                            wtok(e, ffnw_tok["wdD"])
                            wtok(e, ffnw_tok["wu2"])
                        elif fc == 16:
                            wtok(e, ffnw_tok["wg1"])
                    if gidx >= 2:
                        e.wait_ge(sc.h, silu_sc[gidx - 2])
                        e.wait_ge(dv.h, h_dv[gidx - 2])
                    if gidx == 0 and tc == 0:
                        e.wait_ge(pe.h, bfm_pe)
                    i = None
                    for ic in range(NIC):
                        i = e.matmul(gu[:, 0:256],
                                     wgT_b[:, fc * 1024 + ic * 128: fc * 1024 + ic * 128 + 128],
                                     xnc[tc % 2][:, ic * 256:(ic + 1) * 256],
                                     start=(ic == 0), stop=(ic == NIC - 1))
                    pe.inc(i)
                    gate_pe.append(pe.n)
                    i = None
                    for ic in range(NIC):
                        i = e.matmul(uu,
                                     wuT_b[:, fc * 1024 + ic * 128: fc * 1024 + ic * 128 + 128],
                                     xnc[tc % 2][:, ic * 256:(ic + 1) * 256],
                                     start=(ic == 0), stop=(ic == NIC - 1))
                    pe.inc(i)
                    up_pe.append(pe.n)
                    if fc == 0:
                        wtok(e, ctx_ld_tok[tc])
                        if tc >= 1:
                            e.wait_ge(dv.h, fevac_dv[-1])
                        i = None
                        for th in range(2):
                            for dj in range(2):
                                for ic in range(NIC):
                                    i = e.matmul(
                                        PS[th][:, dj * 512:(dj + 1) * 512],
                                        ctx2[tc % 2][:, th * 1024 + ic * 128:
                                                     th * 1024 + ic * 128 + 128],
                                        wb[:, ic * D + dj * 512: ic * D + dj * 512 + 512],
                                        start=(ic == 0), stop=False)
                                i = e.matmul(
                                    PS[th][:, dj * 512:(dj + 1) * 512],
                                    onesb[0:1, 0:128],
                                    biasb[0:1, BOFF["o"] + dj * 512: BOFF["o"] + dj * 512 + 512],
                                    start=False, stop=False)
                        pe.inc(i)
                        ommf_pe.append(pe.n)
                    if fc >= 2:
                        e.wait_ge(dv.h, h_dv[gidx - 2])
                        i = None
                        for th in range(2):
                            for dj in range(2):
                                i = e.matmul(PS[th][:, dj * 512:(dj + 1) * 512],
                                             hbuf[(fc - 2) % 3][:, th * 128:(th + 1) * 128],
                                             wdT_b[:, WDSLOT[fc - 2] * D + dj * 512:
                                                   WDSLOT[fc - 2] * D + dj * 512 + 512],
                                             start=False, stop=False)
                        pe.inc(i)
                        down_pe.append(pe.n)

                @P.scalar
                def _(e, gu=gu, fc=fc, gidx=gidx):
                    e.wait_ge(pe.h, gate_pe[gidx])
                    if gidx >= 2:
                        e.wait_ge(dv.h, h_dv[gidx - 2])
                    i = e.activation(sg[fc % 2][:], gu[:, 0:256], AF.Silu, scale=WSCALE)
                    sc.inc(i)
                    silu_sc.append(sc.n)

                @P.vector
                def _(e, uu=uu, fc=fc, gidx=gidx):
                    e.wait_ge(pe.h, up_pe[gidx])
                    e.wait_ge(sc.h, silu_sc[gidx])
                    if len(down_pe) >= 1 and gidx >= 3:
                        e.wait_ge(pe.h, down_pe[-1])
                    i = e.scalar_tensor_tensor(hbuf[fc % 3][:], sg[fc % 2][:], W2,
                                               uu, ALU.mult, ALU.mult)
                    dv.inc(i)
                    h_dv.append(dv.n)

            @P.tensor
            def _(e, tc=tc):
                e.wait_ge(dv.h, h_dv[-1])
                i = None
                for fcl in (NFC - 2, NFC - 1):
                    for th in range(2):
                        for dj in range(2):
                            i = e.matmul(PS[th][:, dj * 512:(dj + 1) * 512],
                                         hbuf[fcl % 3][:, th * 128:(th + 1) * 128],
                                         wdT_b[:, WDSLOT[fcl] * D + dj * 512:
                                               WDSLOT[fcl] * D + dj * 512 + 512],
                                         start=False, stop=(fcl == NFC - 1))
                pe.inc(i)
                down_pe.append(pe.n)

            for th in range(2):
                @P.vector
                def _(e, tc=tc, th=th):
                    e.wait_ge(pe.h, down_pe[-1])
                    wtok(e, xres_tok[2 * tc + th])
                    if tc >= 1:
                        wtok(e, out_tok[2 * (tc - 1) + th])
                    i = e.tensor_add(obuf[th][:], PS[th][:], xres2[(2 * tc + th) % 2][:])
                    dv.inc(i)
                    fevac_dv.append(dv.n)

                @P.gpsimd
                def _(e, tc=tc, th=th):
                    e.wait_ge(dv.h, fevac_dv[-1])
                    row = tc * 256 + th * 128
                    out_tok.append(stq.dma(e, out_d[row:row + 128, :],
                                           obuf[th][:]))

        _emit_taps(P, tap_d, taps, locals(), dv, sc, pe, stq)

        @P.gpsimd
        def _(e):
            for sem, cnt in stq.all_tokens():
                e.wait_ge(sem, cnt)
    return nc


def _emit_taps(P, tap_d, taps, env, dv, sc, pe, stq):
    if not taps:
        return

    @P.gpsimd
    def _(e):
        e.wait_ge(dv.h, dv.n)
        e.wait_ge(sc.h, sc.n)
        e.wait_ge(pe.h, pe.n)
        for tname, tshape, tdt in taps:
            buf = env[tname]
            stq.dma(e, tap_d[tname], buf[:])
        for sem, cnt in stq.all_tokens():
            e.wait_ge(sem, cnt)



def _make_consts():
    idt = np.eye(128, dtype=ml_dtypes.bfloat16)
    p = np.zeros((128, 128), dtype=np.float32)
    for b in (0, 64):
        for r in range(32):
            p[b + r, b + 32 + r] = -1.0
            p[b + 32 + r, b + r] = 1.0
    ptm = np.ascontiguousarray(p.T).astype(ml_dtypes.bfloat16)
    rows = np.arange(128) % 64
    freqs = rows % 32
    inv = 1.0 / 10000.0 ** (freqs / 32.0)
    t = np.arange(T, dtype=np.float64)
    ang = inv[:, None] * t[None, :]
    cosb = np.cos(ang).astype(np.float32).astype(ml_dtypes.bfloat16)
    sinb = np.sin(ang).astype(np.float32).astype(ml_dtypes.bfloat16)
    ones = np.ones((1, 128), dtype=ml_dtypes.bfloat16)
    return idt, ptm, cosb, sinb, ones


def _layout_ffn(wg, wu, wd):
    def gu(w):
        a = np.asarray(w, dtype=np.int32).T
        a = a.reshape(NIC, 128, NFC, 128)
        a = a.transpose(1, 2, 0, 3)
        return np.ascontiguousarray(a.reshape(128, NFC * NIC * 128))
    d = np.asarray(wd, dtype=np.int32).T
    d = d.reshape(NFC, 128, D).transpose(1, 0, 2)
    return gu(wg), gu(wu), np.ascontiguousarray(d.reshape(128, NFC * D))


_NC_CACHE = {}


def kernel(L0_attn_x, L0_attn_Wq, L0_attn_bq, L0_attn_Wk, L0_attn_bk,
           L0_attn_Wv, L0_attn_bv, L0_attn_Wo, L0_attn_bo,
           L0_ffn_w_gate, L0_ffn_w_up, L0_ffn_w_down):
    x = np.ascontiguousarray(np.asarray(L0_attn_x, dtype=np.float32))
    B = x.shape[0]
    idt, ptm, cosb, sinb, ones = _make_consts()
    common = {
        "wq": np.ascontiguousarray(np.asarray(L0_attn_Wq, dtype=np.float32)),
        "wk": np.ascontiguousarray(np.asarray(L0_attn_Wk, dtype=np.float32)),
        "wv": np.ascontiguousarray(np.asarray(L0_attn_Wv, dtype=np.float32)),
        "wo": np.ascontiguousarray(np.asarray(L0_attn_Wo, dtype=np.float32)),
        "bq": np.asarray(L0_attn_bq, dtype=np.float32).reshape(1, D),
        "bk": np.asarray(L0_attn_bk, dtype=np.float32).reshape(1, D),
        "bv": np.asarray(L0_attn_bv, dtype=np.float32).reshape(1, D),
        "bo": np.asarray(L0_attn_bo, dtype=np.float32).reshape(1, D),
        "idt": None,
    }
    common["wgT"], common["wuT"], common["wdT"] = _layout_ffn(
        L0_ffn_w_gate, L0_ffn_w_up, L0_ffn_w_down)
    common.update({
        "idt": idt, "pt": ptm, "cosb": cosb, "sinb": sinb, "onesb": ones,
    })
    nseq = B // NCORES
    key = ("nc", nseq)
    if key not in _NC_CACHE:
        _NC_CACHE[key] = build_core(n_seq=nseq)
    nc = _NC_CACHE[key]
    in_maps = []
    for c in range(NCORES):
        m = dict(common)
        m["x"] = np.ascontiguousarray(x[c * nseq:(c + 1) * nseq].reshape(nseq * T, D))
        in_maps.append(m)
    res = run_bass_kernel_spmd(nc, in_maps, core_ids=list(range(NCORES)))
    out = np.concatenate([r["out"].reshape(nseq, T, D) for r in res.results], axis=0)
    return out.astype(np.float32)


# version 49
# speedup vs baseline: 1.0204x; 1.0164x over previous
import numpy as np
import ml_dtypes
from contextlib import ExitStack

import concourse.bass as bass
import concourse.mybir as mybir
from concourse.bass_utils import run_bass_kernel_spmd

F32 = mybir.dt.float32
BF16 = mybir.dt.bfloat16
I32 = mybir.dt.int32
AF = mybir.ActivationFunctionType
ALU = mybir.AluOpType
AX = mybir.AxisListType

T = 2048
D = 1024
NHP = 8
FF = 3072
NFC = FF // 128
NIC = D // 128
NKT = T // 128
NQC = T // 512
QSCALE = 0.125
WSCALE = 1.0 / 127.0
W2 = (1.0 / 127.0) ** 2
EPS = 1e-5
NCORES = 8

WDSLOT = {0: 21, 1: 22, 2: 23, 3: 14, 4: 15, 5: 16, 6: 17, 7: 18, 8: 19, 9: 20}
for _f in range(10, NFC):
    WDSLOT[_f] = _f - 10
del _f


class Sem:
    def __init__(self, E, nc, name):
        self.h = E(nc.semaphore(name))
        self.n = 0

    def inc(self, instr, k=1):
        instr.then_inc(self.h, k)
        self.n += k
        return self.n


class DmaQ:

    def __init__(self, E, nc, name, n=3):
        self.sems = [E(nc.semaphore(f"{name}{i}")) for i in range(n)]
        self.cnt = [0] * n
        self.i = 0

    def dma(self, e, out, in_, **kw):
        j = self.i % len(self.sems)
        self.i += 1
        if self.cnt[j]:
            e.wait_ge(self.sems[j], self.cnt[j])
        e.dma_start(out=out, in_=in_, **kw).then_inc(self.sems[j], 16)
        self.cnt[j] += 16
        return (self.sems[j], self.cnt[j])

    def all_tokens(self):
        return [(self.sems[j], self.cnt[j])
                for j in range(len(self.sems)) if self.cnt[j]]


def wtok(e, tok):
    e.wait_ge(tok[0], tok[1])


class Prog:

    _ET = {"sync": mybir.EngineType.SP, "scalar": mybir.EngineType.Activation,
           "vector": mybir.EngineType.DVE, "tensor": mybir.EngineType.PE,
           "gpsimd": mybir.EngineType.Pool}

    def __init__(self, nc, block):
        self.nc = nc
        self.block = block
        self.names = {}

    def _deco(self, engname):
        def deco(f):
            nc = self.nc
            if engname not in self.names:
                getattr(self.block, engname)(f)
                self.names[engname] = nc.m.functions[0].blocks[-1].name
            else:
                with nc.body(self.names[engname],
                             valid_engines=list(self._ET.values()),
                             allow_existing_parent=True):
                    f(getattr(nc, engname))
        return deco

    @property
    def sync(self):
        return self._deco("sync")

    @property
    def scalar(self):
        return self._deco("scalar")

    @property
    def vector(self):
        return self._deco("vector")

    @property
    def tensor(self):
        return self._deco("tensor")

    @property
    def gpsimd(self):
        return self._deco("gpsimd")


def build_core(n_seq=2, taps=(), stop_after=None):
    nc = bass.Bass("TRN2", target_bir_lowering=False)
    NSEQ = n_seq

    x_d = nc.dram_tensor("x", [NSEQ * T, D], F32, kind="ExternalInput").ap()
    w_d = {w: nc.dram_tensor("w" + w, [D, D], F32, kind="ExternalInput").ap()
           for w in ("q", "k", "v", "o")}
    bias_d = {w: nc.dram_tensor("b" + w, [1, D], F32, kind="ExternalInput").ap()
              for w in ("q", "k", "v", "o")}
    wgT_d = nc.dram_tensor("wgT", [128, NIC * FF], I32, kind="ExternalInput").ap()
    wuT_d = nc.dram_tensor("wuT", [128, NIC * FF], I32, kind="ExternalInput").ap()
    wdT_d = nc.dram_tensor("wdT", [128, NFC * D], I32, kind="ExternalInput").ap()
    idt_d = nc.dram_tensor("idt", [128, 128], BF16, kind="ExternalInput").ap()
    pt_d = nc.dram_tensor("pt", [128, 128], BF16, kind="ExternalInput").ap()
    cos_d = nc.dram_tensor("cosb", [128, T], BF16, kind="ExternalInput").ap()
    sin_d = nc.dram_tensor("sinb", [128, T], BF16, kind="ExternalInput").ap()
    ones_d = nc.dram_tensor("onesb", [1, 128], BF16, kind="ExternalInput").ap()
    out_d = nc.dram_tensor("out", [NSEQ * T, D], F32, kind="ExternalOutput").ap()

    xnscr_d = nc.dram_tensor("xnscr", [128, NSEQ * NIC * T], BF16, kind="Internal").ap()
    ctxscr_d = nc.dram_tensor("ctxscr", [128, NSEQ * NKT * 1024], BF16, kind="Internal").ap()

    tap_d = {}
    for tname, tshape, tdt in taps:
        tap_d[tname] = nc.dram_tensor("tap_" + tname, list(tshape), tdt,
                                      kind="ExternalOutput").ap()

    ctx = ExitStack()
    E = ctx.enter_context
    with ctx:
        idt = E(nc.sbuf_tensor("idt_s", [128, 128], BF16))
        pt = E(nc.sbuf_tensor("pt_s", [128, 128], BF16))
        cosb = E(nc.sbuf_tensor("cos_s", [128, T], BF16))
        sinb = E(nc.sbuf_tensor("sin_s", [128, T], BF16))
        onesb = E(nc.sbuf_tensor("ones_s", [1, 128], BF16))
        biasb = E(nc.sbuf_tensor("bias_s", [1, 4 * D], BF16))
        BOFF = {"q": 0, "k": D, "v": 2 * D, "o": 3 * D}
        zr = E(nc.sbuf_tensor("zr", [128, 16], F32))
        bfm = E(nc.sbuf_tensor("bfm", [128, 16], F32))
        statsA = E(nc.sbuf_tensor("statsA", [128, 32], F32))
        statsB = E(nc.sbuf_tensor("statsB", [128, 48], F32))

        PS = [E(nc.psum_tensor(f"PS{i}", [128, 1024], F32)) for i in range(3)]
        PC = [E(nc.psum_tensor(f"PC{i}", [128, 512], F32)) for i in range(2)]
        pOO = PS[0][:, 0:512]

        dv = Sem(E, nc, "dv_p")
        sc = Sem(E, nc, "sc_p")
        pe = Sem(E, nc, "pe_p")
        spq = DmaQ(E, nc, "spq", 3)
        gpq = DmaQ(E, nc, "gpq", 3)
        stq = DmaQ(E, nc, "stq", 4)

        block = E(nc.Block())
        P = Prog(nc, block)

        actx = ExitStack()
        A = actx.enter_context
        wa = A(nc.sbuf_tensor("wa", [128, NIC * D], BF16))
        wb = A(nc.sbuf_tensor("wb", [128, NIC * D], BF16))
        xn_fm = A(nc.sbuf_tensor("xn_fm", [128, NIC * T], BF16))
        k_fm = A(nc.sbuf_tensor("k_fm", [128, NHP * T], BF16))
        q_fm = A(nc.sbuf_tensor("q_fm", [128, NHP * 512], BF16))
        vp = A(nc.sbuf_tensor("vp", [128, NKT * 1040], BF16))
        exa = A(nc.sbuf_tensor("exa", [128, 3072], BF16))
        ctxtm = A(nc.sbuf_tensor("ctxtm", [128, 4 * 1024], BF16))
        ctxfm = [A(nc.sbuf_tensor(f"ctxfm{i}", [128, 1024], BF16)) for i in range(2)]
        xldr = [A(nc.sbuf_tensor(f"xldr{i}", [128, D], F32)) for i in range(2)]
        kraw = [A(nc.sbuf_tensor(f"kraw{i}", [128, 512], BF16)) for i in range(2)]
        tmps = [A(nc.sbuf_tensor(f"tmps{i}", [128, 512], BF16)) for i in range(2)]
        tmpc = [A(nc.sbuf_tensor(f"tmpc{i}", [128, 512], BF16)) for i in range(2)]
        spacer2 = A(nc.sbuf_tensor("spacer2", [128, 3840], BF16))
        obuf = [A(nc.sbuf_tensor(f"obuf{i}", [128, D], F32)) for i in range(2)]
        sg = [A(nc.sbuf_tensor(f"sg{i}", [128, 256], BF16)) for i in range(2)]
        hbuf = [A(nc.sbuf_tensor(f"hbuf{i}", [128, 256], BF16)) for i in range(2)]
        hbuf.append(nc.alloc_sbuf_tensor_at("hbuf2", [128, 256], BF16,
                                            offset=nc.lookup_mloc(biasb).addr))
        exa2 = A(nc.sbuf_tensor("exa2", [128, 1024], BF16))
        exa3 = A(nc.sbuf_tensor("exa3", [128, 1024], BF16))

        def exslot(j, lo, n):
            j = j % 5
            if j < 3:
                return exa[:, j * 1024 + lo: j * 1024 + lo + n]
            return (exa2, exa3)[j - 3][:, lo: lo + n]
        xb = [ctxtm[:, i * D:(i + 1) * D] for i in range(4)]
        sqscr = ctxfm[0]

        def _addr(h):
            return nc.lookup_mloc(h).addr

        assert _addr(k_fm) == _addr(xn_fm) + 32768
        assert _addr(q_fm) == _addr(k_fm) + 32768
        assert _addr(vp) == _addr(q_fm) + 8192
        assert _addr(spacer2) + 7680 == _addr(vp) + 24576 + 49152, (_addr(spacer2), _addr(vp))
        wgT_b = nc.alloc_sbuf_tensor_at("wgT_b", [128, NIC * FF], BF16,
                                        offset=_addr(xn_fm))
        wuT_b = nc.alloc_sbuf_tensor_at("wuT_b", [128, NIC * FF], BF16,
                                        offset=_addr(k_fm) + 16384)
        wdT_b = nc.alloc_sbuf_tensor_at("wdT_b", [128, NFC * D], BF16,
                                        offset=_addr(vp) + 24576)
        xnc = [nc.alloc_sbuf_tensor_at(f"xnc{i}", [128, NIC * 256], BF16,
                                       offset=_addr(wa) + i * 4096) for i in range(2)]
        ctx2 = [nc.alloc_sbuf_tensor_at(f"ctx2_{i}", [128, 2 * 1024], BF16,
                                        offset=_addr(wa) + 8192 + i * 4096) for i in range(2)]
        xres2 = [nc.alloc_sbuf_tensor_at(f"xres2_{i}", [128, D], F32,
                                         offset=_addr((cosb, sinb)[i]))
                 for i in range(2)]
        ffnw_tok = {}

        ctok = {}

        @P.sync
        def _(e):
            for nm, dst, src in (("idt", idt, idt_d), ("pt", pt, pt_d),
                                 ("ones", onesb, ones_d)):
                ctok[nm] = spq.dma(e, dst[:], src)

        @P.gpsimd
        def _(e):
            for w in ("q", "k", "v", "o"):
                ctok["b" + w] = gpq.dma(e, biasb[:, BOFF[w]:BOFF[w] + D], bias_d[w])
            ffnw_tok["wdA"] = gpq.dma(e, wdT_b[:, 21 * D:24 * D], wdT_d[:, 0:3 * D])

        wload_tok = {}

        def load_w(buf, name, key, wait_pe=0, split=None):
            @P.gpsimd
            def _(e):
                if wait_pe:
                    e.wait_ge(pe.h, wait_pe)
                dst = buf[:].rearrange("p (ic o) -> p ic o", ic=NIC)
                src = w_d[key].rearrange("(ic p) o -> p ic o", p=128)
                if split is None:
                    wload_tok[name] = gpq.dma(e, dst, src)
                else:
                    lo = 0
                    for ci, hi in enumerate(split):
                        wload_tok[(name, ci)] = gpq.dma(
                            e, dst[:, :, lo:hi], src[:, :, lo:hi])
                        lo = hi

        load_w(wa, "wk0", "k")
        load_w(wb, "wv0", "v")

        @P.tensor
        def _(e):
            wtok(e, ctok["bq"])
            wtok(e, ctok["bk"])
            wtok(e, ctok["ones"])
            i = None
            for wi, w in enumerate(("q", "k")):
                for oc in range(NHP):
                    i = e.matmul(PS[2][:, wi * 8 + oc: wi * 8 + oc + 1],
                                 biasb[0:1, BOFF[w] + oc * 128: BOFF[w] + oc * 128 + 128],
                                 onesb[0:1, 0:1], start=True, stop=True)
            pe.inc(i)
        bfm_pe = pe.n

        @P.vector
        def _(e):
            e.wait_ge(pe.h, bfm_pe)
            i = e.tensor_copy(bfm[:], PS[2][:, 0:16])
            dv.inc(i)
        bfm_dv = dv.n

        NTCH = NSEQ * T // 256
        xnc_tok = []
        ctx_ld_tok = []
        xres_tok = []

        def _ffn_act_loads(e, u):
            s_idx = (u * 256) // T
            w_idx = (u * 256) % T
            src = xnscr_d[:, s_idx * NIC * T: (s_idx + 1) * NIC * T]\
                .rearrange("p (ic t) -> p ic t", ic=NIC)[:, :, w_idx:w_idx + 256]
            xnc_tok.append(spq.dma(
                e, xnc[u % 2][:].rearrange("p (ic t) -> p ic t", ic=NIC), src))
            wtok(e, ctx_tok[2 * u])
            wtok(e, ctx_tok[2 * u + 1])
            ctx_ld_tok.append(spq.dma(
                e, ctx2[u % 2][:], ctxscr_d[:, u * 2048:(u + 1) * 2048]))

        exp_sc_all = []
        sco_pe_all = []
        pv_pe_all = []
        div_dv_all = []
        ocev_dv_all = []
        otp_pe_all = []
        ctx_tok = {}
        ln_sq_sc = []
        state = {"last_rope_pe": 0, "xn_read_pe": 0, "tpc_pe_last": 0,
                 "stats_sc": 0}

        for s in range(NSEQ):
            ln_x_tok = []
            ln_sq_sc_l = []
            ln_red_dv = []
            ln_nmr_dv = []
            ln_tp_pe = []
            ln_evac_dv = []
            ln_x2_tok = []

            lnring = [xldr[0], xldr[1], obuf[0], obuf[1]]
            for tt in range(NKT):
                par = tt % 4

                @P.sync
                def _(e, tt=tt, par=par, s=s):
                    if s == 0 and tt == 4 and "cos" not in ctok:
                        ctok["cos"] = spq.dma(e, cosb[:], cos_d)
                        ctok["sin"] = spq.dma(e, sinb[:], sin_d)
                    if s > 0 and tt < 4:
                        e.wait_ge(sc.h, state["xldr_sc"])
                        e.wait_ge(dv.h, state["xldr_dv"])
                    if tt >= 4:
                        e.wait_ge(sc.h, ln_sq_sc_l[tt - 4])
                        e.wait_ge(dv.h, ln_red_dv[tt - 4])
                    ln_x_tok.append(spq.dma(
                        e, lnring[par][:],
                        x_d[s * T + tt * 128: s * T + tt * 128 + 128, :]))

                @P.scalar
                def _(e, tt=tt, par=par):
                    wtok(e, ln_x_tok[tt])
                    if tt == 0:
                        if ocev_dv_all:
                            wtok(e, ocev_dv_all[-1])
                        if ctx_tok:
                            for qtg in sorted(ctx_tok)[-2:]:
                                wtok(e, ctx_tok[qtg])
                    else:
                        e.wait_ge(sc.h, ln_sq_sc_l[tt - 1])
                    i = e.activation(sqscr[:], lnring[par][:], AF.Square,
                                     accum_out=statsA[:, 16 + tt:17 + tt])
                    sc.inc(i)
                    ln_sq_sc.append(sc.n)
                    ln_sq_sc_l.append(sc.n)

                @P.vector
                def _(e, tt=tt, par=par):
                    wtok(e, ln_x_tok[tt])
                    if tt == 0 and s > 0:
                        e.wait_ge(sc.h, state["stats_sc"])
                    i = e.reduce_sum(statsA[:, tt:tt + 1], lnring[par][:], axis=AX.X)
                    dv.inc(i)
                    ln_red_dv.append(dv.n)

            @P.vector
            def _(e):
                e.wait_ge(dv.h, ln_red_dv[-1])
                i = e.tensor_scalar_mul(statsB[:, 0:16], statsA[:, 0:16], 1.0 / D)
                dv.inc(i)
                e.wait_ge(dv.h, dv.n)
                i = e.tensor_mul(statsB[:, 16:32], statsB[:, 0:16], statsB[:, 0:16])
                dv.inc(i)
                e.wait_ge(sc.h, ln_sq_sc_l[-1])
                e.wait_ge(dv.h, dv.n)
                i = e.scalar_tensor_tensor(statsA[:, 0:16], statsA[:, 16:32], 1.0 / D,
                                           statsB[:, 16:32], ALU.mult, ALU.subtract)
                dv.inc(i)
                e.wait_ge(dv.h, dv.n)
                i = e.tensor_scalar_add(statsB[:, 16:32], statsA[:, 0:16], EPS)
                dv.inc(i)
            var_dv = dv.n

            @P.scalar
            def _(e, var_dv=var_dv):
                e.wait_ge(dv.h, var_dv)
                i = e.activation(statsA[:, 0:16], statsB[:, 16:32], AF.Ln)
                sc.inc(i)
                e.wait_ge(sc.h, sc.n)
                i = e.activation(statsB[:, 32:48], statsA[:, 0:16], AF.Exp, scale=-0.5)
                sc.inc(i)
            rstd_sc = sc.n
            state["stats_sc"] = rstd_sc

            for tt in range(NKT):
                par = tt % 4

                @P.sync
                def _(e, tt=tt, par=par, s=s):
                    if tt < 4:
                        e.wait_ge(sc.h, ln_sq_sc_l[12 + par])
                        e.wait_ge(dv.h, ln_red_dv[12 + par])
                    else:
                        e.wait_ge(dv.h, ln_nmr_dv[tt - 4])
                    ln_x2_tok.append(spq.dma(
                        e, lnring[par][:],
                        x_d[s * T + tt * 128: s * T + tt * 128 + 128, :]))

                @P.vector
                def _(e, tt=tt, par=par, rstd_sc=rstd_sc):
                    wtok(e, ln_x2_tok[tt])
                    if tt < 4:
                        e.wait_ge(sc.h, rstd_sc)
                        e.wait_ge(pe.h, state["tpc_pe_last"])
                        for qtg in sorted(ctx_tok)[-4:]:
                            wtok(e, ctx_tok[qtg])
                    else:
                        e.wait_ge(pe.h, ln_tp_pe[tt - 4])
                    i = e.tensor_scalar(xb[par][:], lnring[par][:],
                                        statsB[:, tt:tt + 1], statsB[:, 32 + tt:33 + tt],
                                        ALU.subtract, ALU.mult)
                    dv.inc(i)
                    ln_nmr_dv.append(dv.n)

                @P.tensor
                def _(e, tt=tt, par=par, s=s):
                    e.wait_ge(dv.h, ln_nmr_dv[tt])
                    if tt >= 1:
                        wtok(e, ln_evac_dv[tt - 1])
                    else:
                        if ocev_dv_all:
                            wtok(e, ocev_dv_all[-1])
                        if s == 0:
                            wtok(e, ctok["idt"])
                            e.wait_ge(dv.h, bfm_dv)
                    pst = PS[2]
                    i = None
                    for ic in range(NIC):
                        i = e.matmul(pst[:, ic * 128:(ic + 1) * 128],
                                     xb[par][:, ic * 128:(ic + 1) * 128], idt[:],
                                     start=True, stop=True)
                    pe.inc(i)
                    ln_tp_pe.append(pe.n)

                @P.scalar
                def _(e, tt=tt, s=s):
                    e.wait_ge(pe.h, ln_tp_pe[tt])
                    if tt == 0 and s > 0:
                        e.wait_ge(pe.h, state["xn_read_pe"])
                        wtok(e, state[("xnscr_tok", s - 1)])
                    pst = PS[2]
                    dst = xn_fm[:].rearrange("p (ic t) -> p ic t", ic=NIC)[:, :, tt * 128:tt * 128 + 128]
                    src = pst.rearrange("p (ic c) -> p ic c", ic=NIC)
                    i = e.copy(dst, src)
                    sc.inc(i)
                    ln_evac_dv.append((sc.h, sc.n))

            xn_tok = ln_evac_dv[-1]
            state["xldr_sc"] = ln_sq_sc_l[-1]
            state["xldr_dv"] = ln_nmr_dv[-1]

            @P.sync
            def _(e, s=s, xn_tok=xn_tok):
                wtok(e, xn_tok)
                state[("xnscr_tok", s)] = spq.dma(
                    e, xnscr_d[:, s * NIC * T:(s + 1) * NIC * T], xn_fm[:])

            if s == NSEQ - 1 and stop_after is None:
                @P.gpsimd
                def _(e, xsc=state["xldr_sc"], xdv=state["xldr_dv"]):
                    e.wait_ge(sc.h, xsc)
                    e.wait_ge(dv.h, xdv)
                    ffnw_tok["wdB"] = gpq.dma(
                        e, wdT_b[:, 14 * D:17 * D], wdT_d[:, 3 * D:6 * D])

            if stop_after == "ln" and s == NSEQ - 1:
                break

            kv_proj_pe = []
            kv_tmp_dv = []
            kv_rope_pe = []
            kv_evac_dv = []
            v_proj_pe = []
            v_evac_dv = []
            kraw_sc = []
            punit_free = []

            @P.vector
            def _(e, s=s):
                if ("vp_read_pe", s - 1) in state:
                    e.wait_ge(pe.h, state["vp_read_pe", s - 1])
                i = e.memset(vp[:].rearrange("p (kt h c) -> p kt h c",
                                             kt=NKT, h=16)[:, :, :, 64:65], 1.0)
                dv.inc(i)
            vones_dv = dv.n

            def emit_k_rope(jt, s=s):
                par = jt % 2
                pR = PS[1][:, par * 512:(par + 1) * 512]

                @P.tensor
                def _(e, par=par, pR=pR, jt=jt, s=s):
                    e.wait_ge(dv.h, kv_tmp_dv[jt])
                    if jt >= 2:
                        wtok(e, kv_evac_dv[jt - 2])
                    if jt == 0 and s == 0:
                        wtok(e, ctok["pt"])
                    e.matmul(pR, idt[:], tmpc[par][:], start=True, stop=False)
                    i = e.matmul(pR, pt[:], tmps[par][:], start=False, stop=True)
                    pe.inc(i)
                    kv_rope_pe.append(pe.n)

                tc, hp = jt // NHP, jt % NHP

                @P.scalar
                def _(e, hp=hp, tc=tc, pR=pR, jt=jt, s=s):
                    e.wait_ge(pe.h, kv_rope_pe[jt])
                    if s > 0 and jt == 0:
                        e.wait_ge(pe.h, state["kfm_read_pe", s - 1])
                    i = e.copy(k_fm[:, hp * T + tc * 512: hp * T + tc * 512 + 512], pR)
                    sc.inc(i)
                    kv_evac_dv.append((sc.h, sc.n))

            uidx = 0
            kidx = 0
            vidx = 0
            for tc in range(4):
                for u in range(16):
                    par = uidx % 2
                    pP = PS[0][:, par * 512:(par + 1) * 512]
                    is_k = (u % 2 == 1)
                    if is_k:
                        hp = u // 2
                        it = kidx

                        @P.tensor
                        def _(e, hp=hp, tc=tc, pP=pP, it=it, uidx=uidx, s=s,
                              lev=list(ln_evac_dv)):
                            if it == 0:
                                wtok(e, wload_tok[f"wk{s}"])
                                wtok(e, ctok["bk"])
                                if s == 0:
                                    wtok(e, ctok["ones"])
                            wtok(e, lev[4 * tc + 3])
                            if uidx >= 2:
                                wtok(e, punit_free[uidx - 2])
                            i = None
                            for ic in range(NIC):
                                i = e.matmul(pP, wa[:, ic * D + hp * 128: ic * D + hp * 128 + 128],
                                             xn_fm[:, ic * T + tc * 512: ic * T + tc * 512 + 512],
                                             start=(ic == 0), stop=(ic == NIC - 1))
                            pe.inc(i)
                            kv_proj_pe.append(pe.n)

                        @P.scalar
                        def _(e, hp=hp, par=par, pP=pP, it=it, s=s):
                            e.wait_ge(pe.h, kv_proj_pe[it])
                            if it >= 2:
                                e.wait_ge(dv.h, kv_tmp_dv[it - 2])
                            if it == 0 and s == 0:
                                e.wait_ge(dv.h, bfm_dv)
                            i = e.activation(kraw[it % 2][:], pP, AF.Identity,
                                             bias=bfm[:, 8 + hp:9 + hp])
                            sc.inc(i)
                            kraw_sc.append(sc.n)
                            punit_free.append((sc.h, sc.n))

                        @P.vector
                        def _(e, tc=tc, it=it, s=s):
                            e.wait_ge(sc.h, kraw_sc[it])
                            if it >= 2:
                                e.wait_ge(pe.h, kv_rope_pe[it - 2])
                            elif state["last_rope_pe"]:
                                e.wait_ge(pe.h, state["last_rope_pe"])
                            if it == 0 and s == 0:
                                wtok(e, ctok["cos"])
                                wtok(e, ctok["sin"])
                            i = e.tensor_mul(tmps[it % 2][:], kraw[it % 2][:],
                                             sinb[:, tc * 512:(tc + 1) * 512])
                            dv.inc(i)
                            i = e.tensor_mul(tmpc[it % 2][:], kraw[it % 2][:],
                                             cosb[:, tc * 512:(tc + 1) * 512])
                            dv.inc(i)
                            kv_tmp_dv.append(dv.n)

                        if kidx >= 1:
                            emit_k_rope(kidx - 1)
                        kidx += 1
                    else:
                        tt = tc * 4 + u // 4
                        j = (u // 2) % 2
                        it = vidx

                        @P.tensor
                        def _(e, tt=tt, j=j, pP=pP, it=it, uidx=uidx, s=s,
                              lev=list(ln_evac_dv)):
                            if it == 0:
                                wtok(e, wload_tok[f"wv{s}"])
                                wtok(e, ctok["bv"])
                                if s == 0:
                                    wtok(e, ctok["ones"])
                            wtok(e, lev[tt])
                            if uidx >= 2:
                                wtok(e, punit_free[uidx - 2])
                            i = None
                            for ic in range(NIC):
                                i = e.matmul(pP, xn_fm[:, ic * T + tt * 128: ic * T + tt * 128 + 128],
                                             wb[:, ic * D + j * 512: ic * D + j * 512 + 512],
                                             start=(ic == 0), stop=False)
                            i = e.matmul(pP, onesb[0:1, 0:128],
                                         biasb[0:1, BOFF["v"] + j * 512: BOFF["v"] + j * 512 + 512],
                                         start=False, stop=True)
                            pe.inc(i)
                            v_proj_pe.append(pe.n)

                        @P.scalar
                        def _(e, tt=tt, j=j, pP=pP, it=it, vones_dv=vones_dv):
                            e.wait_ge(pe.h, v_proj_pe[it])
                            e.wait_ge(dv.h, vones_dv)
                            dst = vp[:].rearrange("p (kt h c) -> p kt h c",
                                                  kt=NKT, h=16)[:, tt, 8 * j:8 * j + 8, 0:64]
                            src = pP.rearrange("p (h c) -> p h c", h=8)
                            i = e.copy(dst, src)
                            sc.inc(i)
                            v_evac_dv.append((sc.h, sc.n))
                            punit_free.append((sc.h, sc.n))

                        vidx += 1
                    uidx += 1
            emit_k_rope(kidx - 1)
            state["last_rope_pe"] = kv_rope_pe[-1]

            v_done_tok = v_evac_dv[-1]
            load_w(wa, f"wq{s}", "q", wait_pe=kv_proj_pe[-1],
                   split=tuple(range(128, 1025, 128)))
            if s == NSEQ - 1:
                @P.gpsimd
                def _(e):
                    e.wait_ge(pe.h, v_proj_pe[-1])
                    ffnw_tok["wo"] = gpq.dma(
                        e, wb[:].rearrange("p (ic o) -> p ic o", ic=NIC),
                        w_d["o"].rearrange("(ic p) o -> p ic o", p=128))

            if stop_after == "kv" and s == NSEQ - 1:
                break

            state["kv_evac_last"] = kv_evac_dv[-1]
            qp = {}

            def new_qpset(qc_t, prev):
                qp[qc_t] = {"raw": [], "kraw": [], "tmp": [], "rope": [],
                            "evac": [], "prev": prev}

            def emit_qproj_raw(qc_t, h, pc, s=s, sa=False):
                L = qp[qc_t]
                pP2 = (PS[0][:, (h % 2) * 512:(h % 2) * 512 + 512] if sa
                       else PS[2][:, 0:512])
                dep = 2 if sa else 1

                @P.tensor
                def _(e, qc_t=qc_t, h=h, L=L, sa=sa, dep=dep, pP2=pP2, pc=pc):
                    if pc[0] == 0:
                        wtok(e, wload_tok[(f"wq{s}", h)])
                        if h == 0:
                            if sa:
                                wtok(e, v_done_tok)
                            else:
                                if L["prev"]["kraw_dv"]:
                                    e.wait_ge(dv.h, L["prev"]["kraw_dv"])
                                if L["prev"]["ps2_sc"]:
                                    wtok(e, L["prev"]["ps2_sc"])
                        if h >= dep:
                            e.wait_ge(dv.h, L["kraw"][h - dep])
                    i = None
                    for ic in range(pc[0], pc[1]):
                        i = e.matmul(pP2,
                                     wa[:, ic * D + h * 128: ic * D + h * 128 + 128],
                                     xn_fm[:, ic * T + qc_t * 512: ic * T + qc_t * 512 + 512],
                                     start=(ic == 0), stop=(ic == NIC - 1))
                    if pc[1] == NIC:
                        pe.inc(i)
                        L["raw"].append(pe.n)

                @P.vector
                def _(e, qc_t=qc_t, h=h, L=L, pP2=pP2, pc=pc):
                    if pc[1] != NIC:
                        return
                    par = h % 2
                    e.wait_ge(pe.h, L["raw"][h])
                    if h >= 2:
                        e.wait_ge(pe.h, L["rope"][h - 2])
                    elif L["prev"]["rope_pe"]:
                        e.wait_ge(pe.h, L["prev"]["rope_pe"])
                    i = e.scalar_tensor_tensor(tmps[par][:], pP2, bfm[:, h:h + 1],
                                               sinb[:, qc_t * 512:(qc_t + 1) * 512],
                                               ALU.add, ALU.mult)
                    dv.inc(i)
                    i = e.scalar_tensor_tensor(tmpc[par][:], pP2, bfm[:, h:h + 1],
                                               cosb[:, qc_t * 512:(qc_t + 1) * 512],
                                               ALU.add, ALU.mult)
                    dv.inc(i)
                    L["tmp"].append(dv.n)
                    L["kraw"].append(dv.n)

            def emit_qproj_b(qc_t, h, qfm_free, sa=False):
                L = qp[qc_t]
                pR2 = (PS[1][:, (h % 2) * 512:(h % 2) * 512 + 512] if sa
                       else PS[2][:, 512:1024])
                dep = 2 if sa else 1

                @P.tensor
                def _(e, h=h, L=L, sa=sa, dep=dep, pR2=pR2):
                    e.wait_ge(dv.h, L["tmp"][h])
                    if h >= dep:
                        e.wait_ge(dv.h, L["evac"][h - dep])
                    elif sa:
                        wtok(e, kv_evac_dv[-1])
                    elif L["prev"]["evac_dv"]:
                        e.wait_ge(dv.h, L["prev"]["evac_dv"])
                    e.matmul(pR2, idt[:], tmpc[h % 2][:], start=True, stop=False)
                    i = e.matmul(pR2, pt[:], tmps[h % 2][:], start=False, stop=True)
                    pe.inc(i)
                    L["rope"].append(pe.n)

                @P.vector
                def _(e, h=h, L=L, qfm_free=qfm_free, pR2=pR2):
                    e.wait_ge(pe.h, L["rope"][h])
                    if qfm_free:
                        e.wait_ge(pe.h, qfm_free)
                    i = e.tensor_scalar_mul(q_fm[:, h * 512:(h + 1) * 512], pR2, 1.0)
                    dv.inc(i)
                    L["evac"].append(dv.n)

            def emit_ctx_tp(qc_t, qt, ocl, div_tok, s=s, pc=(0, NIC)):
                qtg = (s * NQC + qc_t) * 4 + qt

                @P.tensor
                def _(e, qt=qt, div_tok=div_tok, ocl=ocl, pc=pc):
                    if pc[0] == 0:
                        e.wait_ge(dv.h, div_tok)
                        if qt >= 1:
                            wtok(e, ocl[qt - 1])
                        elif ocev_dv_all:
                            wtok(e, ocev_dv_all[-1])
                        else:
                            wtok(e, ln_evac_dv[-1])
                    i = None
                    for ic in range(pc[0], pc[1]):
                        i = e.matmul(PS[2][:, ic * 128:(ic + 1) * 128],
                                     ctxtm[:, qt * 1024 + ic * 128:
                                           qt * 1024 + ic * 128 + 128],
                                     idt[:], start=True, stop=True)
                    if pc[1] == NIC:
                        pe.inc(i)
                        otp_pe_all.append(pe.n)

                if pc[1] != NIC:
                    return

                @P.vector
                def _(e, qt=qt, qtg=qtg):
                    e.wait_ge(pe.h, otp_pe_all[-1])
                    if (qtg - 4) in ctx_tok:
                        wtok(e, ctx_tok[qtg - 4])
                    i = e.tensor_scalar_mul(ctxtm[:, qt * 1024:(qt + 1) * 1024],
                                            PS[2][:], 1.0)
                    dv.inc(i)
                    tok = (dv.h, dv.n)
                    ocev_dv_all.append(tok)
                    ocl.append(tok)

                @P.sync
                def _(e, qt=qt, qtg=qtg, ocl=ocl):
                    wtok(e, ocl[-1])
                    ctx_tok[qtg] = spq.dma(
                        e, ctxscr_d[:, qtg * 1024:(qtg + 1) * 1024],
                        ctxtm[:, qt * 1024:(qt + 1) * 1024])

                if qt == 3:
                    state["tpc_pe_last"] = otp_pe_all[-1]

            new_qpset(0, {"kraw_dv": 0, "ps2_sc": None,
                          "rope_pe": state["last_rope_pe"], "evac_dv": 0})
            qfm0 = state.get(("kfm_read_pe", s - 1), 0)
            for h in range(NHP):
                emit_qproj_raw(0, h, (0, NIC), sa=True)
                if h >= 1:
                    emit_qproj_b(0, h - 1, qfm0, sa=True)
            emit_qproj_b(0, NHP - 1, qfm0, sa=True)

            for qc in range(NQC):
                tp_ocl = []
                tp_div = div_dv_all[-1] if qc >= 1 else 0
                if s == NSEQ - 1 and qc == NQC - 1 and stop_after is None:
                    @P.gpsimd
                    def _(e, s=s, qpt=qp[NQC - 1]["raw"][-1],
                          qr=qp[NQC - 1]["rope"][-1],
                          xsc=state["xldr_sc"], xdv=state["xldr_dv"]):
                        e.wait_ge(pe.h, qpt)
                        wtok(e, state[("xnscr_tok", s)])
                        ffnw_tok["wg0"] = gpq.dma(
                            e, wgT_b[:, 0:16 * 1024], wgT_d[:, 0:16 * 1024])
                        e.wait_ge(pe.h, qr)
                        e.wait_ge(sc.h, xsc)
                        e.wait_ge(dv.h, xdv)
                        ffnw_tok["wdC"] = gpq.dma(
                            e, wdT_b[:, 17 * D:21 * D], wdT_d[:, 6 * D:10 * D])

                def emit_pv(hp, g, gi):
                    @P.tensor
                    def _(e, hp=hp, g=g, gi=gi):
                        e.wait_ge(sc.h, exp_sc_all[gi])
                        if g == 0 and div_dv_all:
                            e.wait_ge(dv.h, div_dv_all[-1])
                        i = None
                        for h2 in range(2):
                            for qt in range(4):
                                i = e.matmul(
                                    PC[qt // 2][:, (qt % 2) * 130 + h2 * 65:
                                                (qt % 2) * 130 + h2 * 65 + 65],
                                    exslot(gi, h2 * 512 + qt * 128, 128),
                                    vp[:, g * 1040 + (2 * hp + h2) * 65:
                                       g * 1040 + (2 * hp + h2) * 65 + 65],
                                    start=(g == 0 and h2 == 0 and qt % 2 == 0),
                                    stop=(g == NKT - 1 and h2 == 1 and qt % 2 == 1))
                        pe.inc(i)
                        pv_pe_all.append(pe.n)

                def emit_divide(hp_t):
                    @P.vector
                    def _(e, hp=hp_t):
                        e.wait_ge(pe.h, pv_pe_all[-1])
                        if hp == 0:
                            e.wait_ge(pe.h, state["tpc_pe_last"])
                            for qtg in sorted(ctx_tok)[-4:]:
                                wtok(e, ctx_tok[qtg])
                        for i2 in range(2):
                            src = PC[i2][:, 0:260].rearrange("p (q2 h2 c) -> p q2 h2 c",
                                                             q2=2, h2=2)[:, :, :, 64:65]
                            dst = zr[:, 4 * i2:4 * i2 + 4].rearrange("p (a b c) -> p a b c",
                                                                     a=2, b=2)
                            i = e.tensor_copy(dst, src)
                            dv.inc(i)
                        e.wait_ge(dv.h, dv.n)
                        i = e.reciprocal(zr[:, 8:16], zr[:, 0:8])
                        dv.inc(i)
                        e.wait_ge(dv.h, dv.n)
                        for qt in range(4):
                            for h2 in range(2):
                                zcol = 8 + (qt // 2) * 4 + (qt % 2) * 2 + h2
                                i = e.tensor_scalar(
                                    ctxtm[:, qt * 1024 + (2 * hp + h2) * 64:
                                          qt * 1024 + (2 * hp + h2) * 64 + 64],
                                    PC[qt // 2][:, (qt % 2) * 130 + h2 * 65:
                                                (qt % 2) * 130 + h2 * 65 + 64],
                                    zr[:, zcol:zcol + 1], None, ALU.mult)
                                dv.inc(i)
                        div_dv_all.append(dv.n)

                for hp in range(NHP):
                    for g in range(NKT):
                        gi = len(sco_pe_all)

                        @P.tensor
                        def _(e, hp=hp, g=g, gi=gi, qc=qc, v_done_tok=v_done_tok):
                            if gi >= 2:
                                e.wait_ge(sc.h, exp_sc_all[gi - 2])
                            if g == 0:
                                e.wait_ge(dv.h, qp[qc]["evac"][hp])
                            if qc == 0 and hp == 0:
                                if g == 0:
                                    e.wait_ge(dv.h, qp[0]["kraw"][-1])
                                elif g == 1:
                                    e.wait_ge(dv.h, qp[0]["evac"][-1])
                            ps_t = PS[gi % 2]
                            e.matmul(ps_t[:, 0:512],
                                     k_fm[0:64, hp * T + g * 128: hp * T + g * 128 + 128],
                                     q_fm[0:64, hp * 512:(hp + 1) * 512],
                                     start=True, stop=True, tile_position=(0, 0))
                            i = e.matmul(ps_t[:, 512:1024],
                                         k_fm[64:128, hp * T + g * 128: hp * T + g * 128 + 128],
                                         q_fm[64:128, hp * 512:(hp + 1) * 512],
                                         start=True, stop=True, tile_position=(64, 0))
                            pe.inc(i)
                            sco_pe_all.append(pe.n)

                        @P.scalar
                        def _(e, g=g, gi=gi):
                            e.wait_ge(pe.h, sco_pe_all[gi])
                            if gi >= 5:
                                e.wait_ge(pe.h, pv_pe_all[gi - 5])
                            i = e.activation(exslot(gi, 0, 1024),
                                             PS[gi % 2][:], AF.Exp, scale=QSCALE)
                            sc.inc(i)
                            exp_sc_all.append(sc.n)

                        if g == 0 and hp >= 1:
                            emit_pv(hp - 1, NKT - 4, len(sco_pe_all) - 5)
                            emit_pv(hp - 1, NKT - 3, len(sco_pe_all) - 4)
                        elif g == 1 and hp >= 1:
                            emit_pv(hp - 1, NKT - 2, len(sco_pe_all) - 4)
                            emit_pv(hp - 1, NKT - 1, len(sco_pe_all) - 3)
                            emit_divide(hp - 1)
                        elif g == 2 and hp >= 1 and qc + 1 < NQC:
                            emit_qproj_b(qc + 1, hp - 1, sco_pe_all[-4])
                        elif g >= 4:
                            emit_pv(hp, g - 4, gi - 4)
                        if hp == 0 and qc >= 1 and 2 <= g <= 9:
                            for k in (2 * (g - 2), 2 * (g - 2) + 1):
                                emit_ctx_tp(qc - 1, k // 4, tp_ocl, tp_div,
                                            pc=(2 * (k % 4), 2 * (k % 4) + 2))
                        if 9 <= g <= 12 and qc + 1 < NQC:
                            if hp == 0 and g == 9:
                                if qc == 0:
                                    new_qpset(1, {
                                        "kraw_dv": 0,
                                        "ps2_sc": ln_evac_dv[-1],
                                        "rope_pe": qp[0]["rope"][-1],
                                        "evac_dv": 0})
                                else:
                                    new_qpset(qc + 1, {
                                        "kraw_dv": qp[qc]["kraw"][-1],
                                        "ps2_sc": ocev_dv_all[-1],
                                        "rope_pe": qp[qc]["rope"][-1],
                                        "evac_dv": qp[qc]["evac"][-1]})
                            emit_qproj_raw(qc + 1, hp, (2 * (g - 9), 2 * (g - 9) + 2))

                for gt in (4, 3, 2, 1):
                    emit_pv(NHP - 1, NKT - gt, len(sco_pe_all) - gt)
                if qc + 1 < NQC:
                    emit_qproj_b(qc + 1, NHP - 1, sco_pe_all[-1])
                emit_divide(NHP - 1)

                state["kfm_read_pe", s] = sco_pe_all[-1]
                state["vp_read_pe", s] = pv_pe_all[-1]
                if qc == NQC - 1:
                    state["xn_read_pe"] = qp[qc]["raw"][-1]
                    state["last_rope_pe"] = qp[qc]["rope"][-1]

                if s == NSEQ - 1 and qc == NQC - 1 and stop_after is None:
                    @P.gpsimd
                    def _(e, sco=sco_pe_all[-1]):
                        e.wait_ge(pe.h, sco)
                        ffnw_tok["wu0"] = gpq.dma(
                            e, wuT_b[:, 0:8 * 1024], wuT_d[:, 0:8 * 1024])
                        ffnw_tok["wu1"] = gpq.dma(
                            e, wuT_b[:, 8 * 1024:12 * 1024], wuT_d[:, 8 * 1024:12 * 1024])
                        ffnw_tok["wg1"] = gpq.dma(
                            e, wgT_b[:, 16 * 1024:24 * 1024], wgT_d[:, 16 * 1024:24 * 1024])

                    @P.sync
                    def _(e, xnr=state["xn_read_pe"], tmp3=qp[NQC - 1]["tmp"][-1]):
                        e.wait_ge(pe.h, xnr)
                        wtok(e, state[("xnscr_tok", 0)])
                        _ffn_act_loads(e, 0)
                        if NTCH > 1:
                            _ffn_act_loads(e, 1)
                        e.wait_ge(dv.h, tmp3)
                        for th in range(2):
                            xres_tok.append(spq.dma(
                                e, xres2[th][:], x_d[th * 128:(th + 1) * 128, :]))

                if qc == NQC - 1:
                    ocl3 = []
                    dt3 = div_dv_all[-1]
                    for qt in range(4):
                        emit_ctx_tp(qc, qt, ocl3, dt3)

            if s + 1 < NSEQ:
                load_w(wa, f"wk{s + 1}", "k", wait_pe=state["xn_read_pe"])
                load_w(wb, f"wv{s + 1}", "v", wait_pe=v_proj_pe[-1])

        if stop_after in ("ln", "kv"):
            _emit_taps(P, tap_d, taps, locals(), dv, sc, pe, stq)
            actx.close()
            return nc

        @P.gpsimd
        def _(e):
            e.wait_ge(pe.h, pv_pe_all[-1])
            e.wait_ge(sc.h, exp_sc_all[-1])
            e.wait_ge(pe.h, otp_pe_all[-1])
            e.wait_ge(dv.h, dv.n)
            for qtg in sorted(ctx_tok)[-4:]:
                wtok(e, ctx_tok[qtg])
            ffnw_tok["wdD"] = gpq.dma(
                e, wdT_b[:, 0:14 * D], wdT_d[:, 10 * D:24 * D])
            ffnw_tok["wu2"] = gpq.dma(
                e, wuT_b[:, 12 * 1024:24 * 1024], wuT_d[:, 12 * 1024:24 * 1024])

        actx.close()

        attn_pe = pe.n
        attn_dv = dv.n
        attn_sc = sc.n
        sc_base_ffn = sc.n

        ommf_pe = []
        gate_pe = []
        up_pe = []
        down_pe = []
        silu_sc = []
        h_dv = []
        fevac_dv = []
        out_tok = []

        for tc in range(NTCH):
            if tc >= 1:
                @P.sync
                def _(e, tc=tc):
                    u = tc + 1
                    if u < NTCH:
                        e.wait_ge(pe.h, up_pe[(tc - 1) * NFC + NFC - 1])
                        e.wait_ge(pe.h, ommf_pe[tc - 1])
                        _ffn_act_loads(e, u)
                    for th in range(2):
                        j = 2 * tc + th
                        e.wait_ge(dv.h, fevac_dv[j - 2])
                        row = tc * 256 + th * 128
                        xres_tok.append(spq.dma(
                            e, xres2[j % 2][:], x_d[row:row + 128, :]))


            for fc in range(NFC):
                gu = PC[fc % 2]
                uu = PS[2][:, (fc % 2) * 512:(fc % 2) * 512 + 256]
                gidx = tc * NFC + fc

                @P.tensor
                def _(e, tc=tc, fc=fc, gu=gu, uu=uu, gidx=gidx):
                    if fc == 0:
                        if tc == 0:
                            wtok(e, ffnw_tok["wg0"])
                            wtok(e, ffnw_tok["wu0"])
                            wtok(e, ffnw_tok["wo"])
                            wtok(e, ctok["bo"])
                            e.wait_ge(dv.h, div_dv_all[-1])
                            wtok(e, ocev_dv_all[-1])
                        wtok(e, xnc_tok[tc])
                    if tc == 0:
                        if fc == 2:
                            wtok(e, ffnw_tok["wdA"])
                        elif fc == 5:
                            wtok(e, ffnw_tok["wdB"])
                        elif fc == 8:
                            wtok(e, ffnw_tok["wdC"])
                            wtok(e, ffnw_tok["wu1"])
                        elif fc == 12:
                            wtok(e, ffnw_tok["wdD"])
                            wtok(e, ffnw_tok["wu2"])
                        elif fc == 16:
                            wtok(e, ffnw_tok["wg1"])
                    if gidx >= 2:
                        e.wait_ge(sc.h, silu_sc[gidx - 2])
                        e.wait_ge(dv.h, h_dv[gidx - 2])
                    if gidx == 0 and tc == 0:
                        e.wait_ge(pe.h, bfm_pe)
                    i = None
                    for ic in range(NIC):
                        i = e.matmul(gu[:, 0:256],
                                     wgT_b[:, fc * 1024 + ic * 128: fc * 1024 + ic * 128 + 128],
                                     xnc[tc % 2][:, ic * 256:(ic + 1) * 256],
                                     start=(ic == 0), stop=(ic == NIC - 1))
                    pe.inc(i)
                    gate_pe.append(pe.n)
                    i = None
                    for ic in range(NIC):
                        i = e.matmul(uu,
                                     wuT_b[:, fc * 1024 + ic * 128: fc * 1024 + ic * 128 + 128],
                                     xnc[tc % 2][:, ic * 256:(ic + 1) * 256],
                                     start=(ic == 0), stop=(ic == NIC - 1))
                    pe.inc(i)
                    up_pe.append(pe.n)
                    if fc == 0:
                        wtok(e, ctx_ld_tok[tc])
                        if tc >= 1:
                            e.wait_ge(dv.h, fevac_dv[-1])
                        i = None
                        for th in range(2):
                            for dj in range(2):
                                for ic in range(NIC):
                                    i = e.matmul(
                                        PS[th][:, dj * 512:(dj + 1) * 512],
                                        ctx2[tc % 2][:, th * 1024 + ic * 128:
                                                     th * 1024 + ic * 128 + 128],
                                        wb[:, ic * D + dj * 512: ic * D + dj * 512 + 512],
                                        start=(ic == 0), stop=False)
                                i = e.matmul(
                                    PS[th][:, dj * 512:(dj + 1) * 512],
                                    onesb[0:1, 0:128],
                                    biasb[0:1, BOFF["o"] + dj * 512: BOFF["o"] + dj * 512 + 512],
                                    start=False, stop=False)
                        pe.inc(i)
                        ommf_pe.append(pe.n)
                    if fc >= 2:
                        e.wait_ge(dv.h, h_dv[gidx - 2])
                        i = None
                        for th in range(2):
                            for dj in range(2):
                                i = e.matmul(PS[th][:, dj * 512:(dj + 1) * 512],
                                             hbuf[(fc - 2) % 3][:, th * 128:(th + 1) * 128],
                                             wdT_b[:, WDSLOT[fc - 2] * D + dj * 512:
                                                   WDSLOT[fc - 2] * D + dj * 512 + 512],
                                             start=False, stop=False)
                        pe.inc(i)
                        down_pe.append(pe.n)

                @P.scalar
                def _(e, gu=gu, fc=fc, gidx=gidx):
                    e.wait_ge(pe.h, gate_pe[gidx])
                    if gidx >= 2:
                        e.wait_ge(dv.h, h_dv[gidx - 2])
                    i = e.activation(sg[fc % 2][:], gu[:, 0:256], AF.Silu, scale=WSCALE)
                    sc.inc(i)
                    silu_sc.append(sc.n)

                @P.vector
                def _(e, uu=uu, fc=fc, gidx=gidx):
                    e.wait_ge(pe.h, up_pe[gidx])
                    e.wait_ge(sc.h, silu_sc[gidx])
                    if len(down_pe) >= 1 and gidx >= 3:
                        e.wait_ge(pe.h, down_pe[-1])
                    i = e.scalar_tensor_tensor(hbuf[fc % 3][:], sg[fc % 2][:], W2,
                                               uu, ALU.mult, ALU.mult)
                    dv.inc(i)
                    h_dv.append(dv.n)

            @P.tensor
            def _(e, tc=tc):
                e.wait_ge(dv.h, h_dv[-1])
                i = None
                for fcl in (NFC - 2, NFC - 1):
                    for th in range(2):
                        for dj in range(2):
                            i = e.matmul(PS[th][:, dj * 512:(dj + 1) * 512],
                                         hbuf[fcl % 3][:, th * 128:(th + 1) * 128],
                                         wdT_b[:, WDSLOT[fcl] * D + dj * 512:
                                               WDSLOT[fcl] * D + dj * 512 + 512],
                                         start=False, stop=(fcl == NFC - 1))
                pe.inc(i)
                down_pe.append(pe.n)

            for th in range(2):
                @P.vector
                def _(e, tc=tc, th=th):
                    e.wait_ge(pe.h, down_pe[-1])
                    wtok(e, xres_tok[2 * tc + th])
                    if tc >= 1:
                        wtok(e, out_tok[2 * (tc - 1) + th])
                    i = e.tensor_add(obuf[th][:], PS[th][:], xres2[(2 * tc + th) % 2][:])
                    dv.inc(i)
                    fevac_dv.append(dv.n)

                @P.gpsimd
                def _(e, tc=tc, th=th):
                    e.wait_ge(dv.h, fevac_dv[-1])
                    row = tc * 256 + th * 128
                    out_tok.append(stq.dma(e, out_d[row:row + 128, :],
                                           obuf[th][:]))

        _emit_taps(P, tap_d, taps, locals(), dv, sc, pe, stq)

        @P.gpsimd
        def _(e):
            for sem, cnt in stq.all_tokens():
                e.wait_ge(sem, cnt)
    return nc


def _emit_taps(P, tap_d, taps, env, dv, sc, pe, stq):
    if not taps:
        return

    @P.gpsimd
    def _(e):
        e.wait_ge(dv.h, dv.n)
        e.wait_ge(sc.h, sc.n)
        e.wait_ge(pe.h, pe.n)
        for tname, tshape, tdt in taps:
            buf = env[tname]
            stq.dma(e, tap_d[tname], buf[:])
        for sem, cnt in stq.all_tokens():
            e.wait_ge(sem, cnt)



def _make_consts():
    idt = np.eye(128, dtype=ml_dtypes.bfloat16)
    p = np.zeros((128, 128), dtype=np.float32)
    for b in (0, 64):
        for r in range(32):
            p[b + r, b + 32 + r] = -1.0
            p[b + 32 + r, b + r] = 1.0
    ptm = np.ascontiguousarray(p.T).astype(ml_dtypes.bfloat16)
    rows = np.arange(128) % 64
    freqs = rows % 32
    inv = 1.0 / 10000.0 ** (freqs / 32.0)
    t = np.arange(T, dtype=np.float64)
    ang = inv[:, None] * t[None, :]
    cosb = np.cos(ang).astype(np.float32).astype(ml_dtypes.bfloat16)
    sinb = np.sin(ang).astype(np.float32).astype(ml_dtypes.bfloat16)
    ones = np.ones((1, 128), dtype=ml_dtypes.bfloat16)
    return idt, ptm, cosb, sinb, ones


def _layout_ffn(wg, wu, wd):
    def gu(w):
        a = np.asarray(w, dtype=np.int32).T
        a = a.reshape(NIC, 128, NFC, 128)
        a = a.transpose(1, 2, 0, 3)
        return np.ascontiguousarray(a.reshape(128, NFC * NIC * 128))
    d = np.asarray(wd, dtype=np.int32).T
    d = d.reshape(NFC, 128, D).transpose(1, 0, 2)
    return gu(wg), gu(wu), np.ascontiguousarray(d.reshape(128, NFC * D))


_NC_CACHE = {}


def kernel(L0_attn_x, L0_attn_Wq, L0_attn_bq, L0_attn_Wk, L0_attn_bk,
           L0_attn_Wv, L0_attn_bv, L0_attn_Wo, L0_attn_bo,
           L0_ffn_w_gate, L0_ffn_w_up, L0_ffn_w_down):
    x = np.ascontiguousarray(np.asarray(L0_attn_x, dtype=np.float32))
    B = x.shape[0]
    idt, ptm, cosb, sinb, ones = _make_consts()
    common = {
        "wq": np.ascontiguousarray(np.asarray(L0_attn_Wq, dtype=np.float32)),
        "wk": np.ascontiguousarray(np.asarray(L0_attn_Wk, dtype=np.float32)),
        "wv": np.ascontiguousarray(np.asarray(L0_attn_Wv, dtype=np.float32)),
        "wo": np.ascontiguousarray(np.asarray(L0_attn_Wo, dtype=np.float32)),
        "bq": np.asarray(L0_attn_bq, dtype=np.float32).reshape(1, D),
        "bk": np.asarray(L0_attn_bk, dtype=np.float32).reshape(1, D),
        "bv": np.asarray(L0_attn_bv, dtype=np.float32).reshape(1, D),
        "bo": np.asarray(L0_attn_bo, dtype=np.float32).reshape(1, D),
        "idt": None,
    }
    common["wgT"], common["wuT"], common["wdT"] = _layout_ffn(
        L0_ffn_w_gate, L0_ffn_w_up, L0_ffn_w_down)
    common.update({
        "idt": idt, "pt": ptm, "cosb": cosb, "sinb": sinb, "onesb": ones,
    })
    nseq = B // NCORES
    key = ("nc", nseq)
    if key not in _NC_CACHE:
        _NC_CACHE[key] = build_core(n_seq=nseq)
    nc = _NC_CACHE[key]
    in_maps = []
    for c in range(NCORES):
        m = dict(common)
        m["x"] = np.ascontiguousarray(x[c * nseq:(c + 1) * nseq].reshape(nseq * T, D))
        in_maps.append(m)
    res = run_bass_kernel_spmd(nc, in_maps, core_ids=list(range(NCORES)))
    out = np.concatenate([r["out"].reshape(nseq, T, D) for r in res.results], axis=0)
    return out.astype(np.float32)
